# version 09
# baseline (speedup 1.0000x reference)
import numpy as np
import ml_dtypes
from contextlib import ExitStack
import concourse.bass as bass
import concourse.mybir as mybir
from concourse.bass_utils import run_bass_kernel_spmd

F32 = mybir.dt.float32
BF16 = mybir.dt.bfloat16
I32 = mybir.dt.int32
AF = mybir.ActivationFunctionType
ALU = mybir.AluOpType
AXX = mybir.AxisListType.X
BFD = ml_dtypes.bfloat16

DM = 1024
H = 16
DH = 64
HID = 3072
CT = 128
SC = 1.0 / 127.0
NCORES = 8


class Tok:
    __slots__ = ("last_w", "readers", "name")

    def __init__(self, name=""):
        self.last_w = None
        self.readers = []
        self.name = name


class Prog:
    DMA_ENGS = ("sync", "pool")
    SELF_SKIP = ("sync", "pool")
    ALL_MS = False
    DENSE_ENGS = ("dve", "act")

    NLANES = 8

    def __init__(self, nc):
        self.nc = nc
        self.engs = {
            "pe": nc.tensor, "dve": nc.vector, "act": nc.scalar,
            "sync": nc.sync, "pool": nc.gpsimd,
        }
        self.sem = {e: nc.alloc_semaphore(name=f"S_{e}")
                    for e in self.engs if e not in self.DMA_ENGS}
        self.lanes = {e: [nc.alloc_semaphore(name=f"S_{e}{i}")
                          for i in range(self.NLANES)] for e in self.DMA_ENGS}
        self.mode = "record"

    def _reset(self):
        self.idx = {e: 0 for e in self.engs}

    def emit(self, eng, fn, reads=(), writes=(), inc=None):
        deps = {}
        for tk in reads:
            if tk.last_w is not None:
                e2, v = tk.last_w
                if deps.get(e2, 0) < v:
                    deps[e2] = v
        for tk in writes:
            if tk.last_w is not None:
                e2, v = tk.last_w
                if deps.get(e2, 0) < v:
                    deps[e2] = v
            for (e2, v) in tk.readers:
                if deps.get(e2, 0) < v:
                    deps[e2] = v
        if self.mode == "record":
            self.sig.append((eng, tuple(sorted(deps.items()))))
            for e2, v in deps.items():
                if e2 == eng and eng in self.SELF_SKIP:
                    continue
                self.ms[e2].add(v)
        else:
            exp = self.sig[self.signo]
            got = (eng, tuple(sorted(deps.items())))
            assert exp == got, f"record/replay divergence at op {self.signo}: {exp} vs {got}"
            self.signo += 1
            self._emit_waits(eng, deps)
        self.idx[eng] += 1
        me = (eng, self.idx[eng])
        if self.mode == "replay":
            self._emit_op(eng, fn, self.idx[eng])
        for tk in reads:
            tk.readers.append(me)
        for tk in writes:
            tk.last_w = me
            tk.readers = []

    NOGRP = False

    def _emit_waits(self, eng, deps):
        for e2 in sorted(deps):
            if e2 == eng and eng in self.SELF_SKIP:
                continue
            v = deps[e2]
            if e2 in self.DMA_ENGS:
                lane = (v - 1) % self.NLANES
                val = 16 * ((v - 1) // self.NLANES + 1)
                key = (e2, lane)
                if val > self.waited[eng].get(key, 0):
                    self.engs[eng].wait_ge(self.lanes[e2][lane], val)
                    self.waited[eng][key] = val
            else:
                if self.msno[e2][v] > self.waited[eng].get(e2, 0):
                    self.engs[eng].wait_ge(self.sem[e2], self.msno[e2][v])
                    self.waited[eng][e2] = self.msno[e2][v]

    def _emit_op(self, eng, fn, ordinal):
        if eng in self.DMA_ENGS:
            lane = (ordinal - 1) % self.NLANES
            use = (ordinal - 1) // self.NLANES + 1
            if use >= 2:
                key = (eng, lane)
                val = 16 * (use - 1)
                if val > self.waited[eng].get(key, 0):
                    self.engs[eng].wait_ge(self.lanes[eng][lane], val)
                    self.waited[eng][key] = val
            ins = fn(self.engs[eng])
            ins.then_inc(self.lanes[eng][lane], 16)
        else:
            ins = fn(self.engs[eng])
            if ordinal in self.ms[eng]:
                ins.then_inc(self.sem[eng], self.unit[eng])
        return ins

    def emit_group(self, eng, fns, reads=(), writes=()):
        if self.NOGRP:
            for fn in fns:
                self.emit(eng, fn, reads=reads, writes=writes)
            return
        deps = {}
        for tk in list(reads) + list(writes):
            if tk.last_w is not None:
                e2, v = tk.last_w
                if deps.get(e2, 0) < v:
                    deps[e2] = v
        for tk in writes:
            for (e2, v) in tk.readers:
                if deps.get(e2, 0) < v:
                    deps[e2] = v
        if self.mode == "record":
            self.sig.append((eng, tuple(sorted(deps.items())), len(fns)))
            for e2, v in deps.items():
                if e2 == eng and eng in self.SELF_SKIP:
                    continue
                self.ms[e2].add(v)
        else:
            exp = self.sig[self.signo]
            got = (eng, tuple(sorted(deps.items())), len(fns))
            assert exp == got, f"record/replay divergence at {self.signo}: {exp} vs {got}"
            self.signo += 1
            self._emit_waits(eng, deps)
        for fn in fns:
            self.idx[eng] += 1
            if self.mode == "replay":
                self._emit_op(eng, fn, self.idx[eng])
        me = (eng, self.idx[eng])
        for tk in reads:
            tk.readers.append(me)
        for tk in writes:
            tk.last_w = me
            tk.readers = []

    def _wait_spec(self, eng, deps):
        if self.mode == "record":
            for e2, v in deps.items():
                self.ms[e2].add(v)
        else:
            for e2 in sorted(deps):
                v = deps[e2]
                if e2 in self.DMA_ENGS:
                    r = (v - 1) % self.NLANES
                    for lane in range(self.NLANES):
                        o = v - ((r - lane) % self.NLANES)
                        if o < 1:
                            continue
                        val = 16 * ((o - 1) // self.NLANES + 1)
                        key = (e2, lane)
                        if val > self.waited[eng].get(key, 0):
                            self.engs[eng].wait_ge(self.lanes[e2][lane], val)
                            self.waited[eng][key] = val
                else:
                    if self.msno[e2][v] > self.waited[eng].get(e2, 0):
                        self.engs[eng].wait_ge(self.sem[e2], self.msno[e2][v])
                        self.waited[eng][e2] = self.msno[e2][v]

    def barrier(self):
        for e in self.engs:
            deps = {e2: self.idx[e2] for e2 in self.engs
                    if e2 != e and self.idx[e2] > 0}
            self._wait_spec(e, deps)

    def finish(self):
        deps = {e2: self.idx[e2] for e2 in self.engs
                if e2 != "sync" and self.idx[e2] > 0}
        self._wait_spec("sync", deps)
        if self.idx["sync"] > 0:
            self._wait_spec("pool", {"sync": self.idx["sync"]})

    def run(self, body):
        self.mode = "record"
        self._reset()
        self.ms = {e: set() for e in self.engs}
        self.sig = []
        self.signo = 0
        body()
        for e in self.engs:
            if self.idx[e] > 0:
                self.ms[e].add(self.idx[e])
        for e in self.DMA_ENGS:
            self.ms[e] = set(range(1, self.idx[e] + 1))
        if self.ALL_MS:
            for e in self.engs:
                self.ms[e] = set(range(1, self.idx[e] + 1))
        for e in self.DENSE_ENGS:
            self.ms[e] = set(range(1, self.idx[e] + 1))
        self.msno = {}
        for e in self.engs:
            cnt = 0
            arr = [0] * (self.idx[e] + 1)
            for i in range(1, self.idx[e] + 1):
                if i in self.ms[e]:
                    cnt += 1
                arr[i] = cnt
            self.msno[e] = arr
        self.unit = {e: 1 for e in self.engs}
        self.waited = {e: {} for e in self.engs}
        self.mode = "replay"
        self._reset()
        body()


def build_nc(n_seq=2, tps=32, dbg=False, skip_p2=False, skip_p1=False):
    n_tiles = n_seq * tps
    NR = n_tiles * CT

    nc = bass.Bass("TRN2", target_bir_lowering=False, debug=False)
    dbg_d = {}
    if dbg:
        for nm, sh, dt in (("dbg_x1", [NR, 1024], F32), ("dbg_st", [128, 2 * n_tiles], F32), ("dbg_x1d", [NR, 1024], F32)):
            dbg_d[nm] = nc.dram_tensor(nm, sh, dt, kind="ExternalOutput").ap()

    x_d = nc.dram_tensor("x", [NR, DM], F32, kind="ExternalInput").ap()
    wq_d = nc.dram_tensor("wq", [DM, DM], F32, kind="ExternalInput").ap()
    wk_d = nc.dram_tensor("wk", [DM, DM], F32, kind="ExternalInput").ap()
    wv_d = nc.dram_tensor("wv", [DM, DM], F32, kind="ExternalInput").ap()
    wb_d = nc.dram_tensor("wb", [DM, H], F32, kind="ExternalInput").ap()
    wo_d = nc.dram_tensor("wo", [DM, DM], F32, kind="ExternalInput").ap()
    wg_d = nc.dram_tensor("wg", [HID, DM], I32, kind="ExternalInput").ap()
    wu_d = nc.dram_tensor("wu", [HID, DM], I32, kind="ExternalInput").ap()
    wd_d = nc.dram_tensor("wd", [DM, HID], I32, kind="ExternalInput").ap()
    bb_d = nc.dram_tensor("bbc", [128, H], F32, kind="ExternalInput").ap()
    id_d = nc.dram_tensor("cident", [128, 128], BF16, kind="ExternalInput").ap()
    mk_d = nc.dram_tensor("cmasks", [128, 3 * 512], BF16, kind="ExternalInput").ap()
    out_d = nc.dram_tensor("out", [NR, DM], F32, kind="ExternalOutput").ap()
    x1_d = nc.dram_tensor("x1scr", [NR, DM], F32, kind="Internal").ap()
    wgb_d = nc.dram_tensor("wg_bf", [HID, DM], BF16, kind="Internal").ap()
    wub_d = nc.dram_tensor("wu_bf", [HID, DM], BF16, kind="Internal").ap()
    wdb_d = nc.dram_tensor("wd_bf", [DM, HID], BF16, kind="Internal").ap()

    st = ExitStack()
    with st:
        ident = st.enter_context(nc.sbuf_tensor("ident", [128, 128], BF16))
        bb_bc = st.enter_context(nc.sbuf_tensor("bb_bc", [128, H], F32))
        wgT = st.enter_context(nc.sbuf_tensor("wgT", [128, 8 * HID], BF16))
        negm2 = st.enter_context(nc.sbuf_tensor("negm2", [128, n_tiles], F32))
        rstd2 = st.enter_context(nc.sbuf_tensor("rstd2", [128, n_tiles], F32))
        lneps = st.enter_context(nc.sbuf_tensor("lneps", [128, 1], F32))

        pk = st.enter_context(nc.psum_tensor("pk", [128, 1024], F32))
        pv = st.enter_context(nc.psum_tensor("pv", [128, 1024], F32))
        pq = st.enter_context(nc.psum_tensor("pq", [128, 1024], F32))
        p7 = st.enter_context(nc.psum_tensor("p7", [128, 512], F32))
        pT = st.enter_context(nc.psum_tensor("pT", [128, 1024], BF16))

        st1 = ExitStack()
        masks = st1.enter_context(nc.sbuf_tensor("masks", [128, 3 * 512], BF16))
        mlo = masks[:, 0:512]
        mup = masks[:, 512:1024]
        mui = masks[:, 1024:1536]
        wq_s = st1.enter_context(nc.sbuf_tensor("wq_s", [128, 8 * 1024], BF16))
        wk_s = st1.enter_context(nc.sbuf_tensor("wk_s", [128, 8 * 1024], BF16))
        wv_s = st1.enter_context(nc.sbuf_tensor("wv_s", [128, 8 * 1024], BF16))
        wo_s = st1.enter_context(nc.sbuf_tensor("wo_s", [128, 8 * 1024], BF16))
        wb_s = st1.enter_context(nc.sbuf_tensor("wb_s", [128, 8 * H], BF16))
        x_in = [st1.enter_context(nc.sbuf_tensor(f"x_in{i}", [128, 1024], F32)) for i in range(2)]
        x1_s = [st1.enter_context(nc.sbuf_tensor(f"x1_s{i}", [128, 1024], F32)) for i in range(2)]
        xhat = st1.enter_context(nc.sbuf_tensor("xhat", [128, 1024], BF16))
        xhfm2 = [st1.enter_context(nc.sbuf_tensor(f"xhfm{i}", [128, 1024], BF16)) for i in range(2)]
        q_tm = st1.enter_context(nc.sbuf_tensor("q_tm", [128, 1024], BF16))
        k_tm = st1.enter_context(nc.sbuf_tensor("k_tm", [128, 1024], BF16))
        ktil = st1.enter_context(nc.sbuf_tensor("ktil", [128, 1024], BF16))
        vt_tm = st1.enter_context(nc.sbuf_tensor("vt_tm", [128, 1024], BF16))
        q_fm = st1.enter_context(nc.sbuf_tensor("q_fm", [128, 1024], BF16))
        k_fm = st1.enter_context(nc.sbuf_tensor("k_fm", [128, 1024], BF16))
        kt_fm = st1.enter_context(nc.sbuf_tensor("kt_fm", [128, 1024], BF16))
        sqs = st1.enter_context(nc.sbuf_tensor("sqs", [128, 1024], F32))
        o_fm = st1.enter_context(nc.sbuf_tensor("o_fm", [128, 1024], BF16))
        S_f = st1.enter_context(nc.sbuf_tensor("S_f", [128, 512], F32))
        S_b = st1.enter_context(nc.sbuf_tensor("S_b", [128, 512], BF16))
        P_s = st1.enter_context(nc.sbuf_tensor("P_s", [128, 2048], BF16))
        PT_s = st1.enter_context(nc.sbuf_tensor("PT_s", [128, 2048], BF16))
        MT_s = st1.enter_context(nc.sbuf_tensor("MT_s", [128, 1024], BF16))
        tmp_s = st1.enter_context(nc.sbuf_tensor("tmp_s", [128, 1024], BF16))
        sm = st1.enter_context(nc.sbuf_tensor("sm", [128, 32], F32))
        ks_s = st1.enter_context(nc.sbuf_tensor("ks_s", [128, 4 * H], F32))
        sb2 = st1.enter_context(nc.sbuf_tensor("sb2", [128, 3 * H], F32))
        st1.close()

        msum = sm[:, 0:1]; negmean = sm[:, 1:2]; m2 = sm[:, 2:3]
        veps = sm[:, 3:4]; lnv = sm[:, 4:5]; rstd = sm[:, 5:6]
        sqsum = sm[:, 6:7]
        m2sum = sm[:, 8:9]; sq2sum = sm[:, 9:10]; m22 = sm[:, 10:11]
        veps2 = sm[:, 11:12]; lnv2 = sm[:, 12:13]
        ksq = ks_s[:, 0:H]; ksq2 = ks_s[:, H:2 * H]
        rr = ks_s[:, 2 * H:3 * H]
        blin = ks_s[:, 3 * H:4 * H]
        beta_v = sb2[:, 0:H]; s1c = sb2[:, H:2 * H]; s2c = sb2[:, 2 * H:3 * H]

        st2 = ExitStack()
        wuT = st2.enter_context(nc.sbuf_tensor("wuT", [128, 8 * HID], BF16))
        wdT = st2.enter_context(nc.sbuf_tensor("wdT", [128, 24 * 1024], BF16))
        x1_in = [st2.enter_context(nc.sbuf_tensor(f"x1i{i}", [128, 1024], F32)) for i in range(2)]
        x1_r = [st2.enter_context(nc.sbuf_tensor(f"x1r{i}", [128, 1024], F32)) for i in range(2)]
        xh2 = st2.enter_context(nc.sbuf_tensor("xh2", [128, 1024], BF16))
        xh2fm = st2.enter_context(nc.sbuf_tensor("xh2fm", [128, 8 * 512], BF16))
        h_fm = st2.enter_context(nc.sbuf_tensor("h_fm", [128, 24 * 512], BF16))
        g_s = [st2.enter_context(nc.sbuf_tensor(f"g_s{i}", [128, 512], BF16)) for i in range(2)]
        y_sb = [st2.enter_context(nc.sbuf_tensor(f"y_sb{i}", [128, 1024], F32)) for i in range(2)]
        st2.close()

        def fmv(tens, h):
            off = (h % 2) * 64
            p = h // 2
            return tens[off:off + 64, p * 128:(p + 1) * 128]

        def Sbv(h):
            off = (h % 2) * 64
            p = h // 2
            return S_b[off:off + 64, p * 64:(p + 1) * 64]

        with nc.Block():
            P = Prog(nc)

            def body():
                E = P.emit
                t_ident = Tok(); t_masks = Tok(); t_bb = Tok()
                t_wgT = Tok(); t_negm2 = Tok(); t_rstd2 = Tok()
                t_lneps = Tok()
                t_pk0, t_pk1 = Tok(), Tok()
                t_pv0, t_pv1 = Tok(), Tok()
                t_pq0, t_pq1 = Tok(), Tok()
                t_p7 = Tok(); t_pT = Tok()
                t_wgb = Tok(); t_wub = Tok(); t_wdb = Tok()
                t_x1s = [Tok() for _ in range(n_tiles)]
                t_wq, t_wk, t_wv, t_wo, t_wb = Tok(), Tok(), Tok(), Tok(), Tok()
                t_xin = [Tok(), Tok()]
                t_x1sb = [Tok(), Tok()]
                t_xhat = Tok(); t_xhfm2 = [Tok(), Tok()]
                t_qtm = Tok(); t_ktm = Tok(); t_ktil = Tok()
                t_vttm = Tok()
                t_qfm = Tok(); t_kfm = Tok(); t_ktfm = Tok()
                t_sqs = Tok(); t_ofm = Tok()
                t_sm = Tok(); t_sm2 = Tok(); t_ks = Tok(); t_sb2 = Tok()
                t_Sf = [Tok() for _ in range(8)]
                t_Sb = [Tok() for _ in range(8)]
                t_P = [Tok() for _ in range(4)]
                t_PT = [Tok() for _ in range(4)]
                t_MT = [Tok() for _ in range(2)]
                t_tmp = [Tok() for _ in range(4)]

                E("sync", lambda e: e.dma_start(out=ident[:, :], in_=id_d[:, :]),
                  writes=[t_ident])
                E("sync", lambda e: e.dma_start(out=bb_bc[:, :], in_=bb_d[:, :]),
                  writes=[t_bb])
                E("sync", lambda e: e.dma_start(out=masks[:, :], in_=mk_d[:, :]),
                  writes=[t_masks])
                E("dve", lambda e: e.memset(lneps[:, :], 1e-5), writes=[t_lneps])

                def wload(dst, src_d, tok, n):
                    E("pool", lambda e: e.dma_start(
                        out=dst[:, :].rearrange("p (c n) -> p c n", n=n),
                        in_=src_d.rearrange("(c p) n -> p c n", p=128)),
                      writes=[tok])
                wload(wq_s, wq_d, t_wq, 1024)
                wload(wk_s, wk_d, t_wk, 1024)
                wload(wv_s, wv_d, t_wv, 1024)
                wload(wb_s, wb_d, t_wb, H)
                wload(wo_s, wo_d, t_wo, 1024)
                if not getattr(Prog, "NO_FFNPREP", False):
                    E("pool", lambda e: e.dma_start(out=wgb_d[:, :], in_=wg_d[:, :]),
                      writes=[t_wgb])
                    E("pool", lambda e: e.dma_start(out=wub_d[:, :], in_=wu_d[:, :]),
                      writes=[t_wub])
                    E("pool", lambda e: e.dma_start(out=wdb_d[:, :], in_=wd_d[:, :]),
                      writes=[t_wdb])

                E("dve", lambda e: e.memset(S_f[:, :], 0.0), writes=t_Sf)
                E("dve", lambda e: e.memset(S_b[:, :], 0.0), writes=t_Sb)

                E("sync", lambda e: e.dma_start(out=x_in[0][:, :], in_=x_d[0:128, :]),
                  writes=[t_xin[0]])

                wgT_calls = list(range(8))

                GP = {0: (t_pk0, pk, 0), 1: (t_pk1, pk, 512)}
                GV = {0: (t_pv0, pv, 0), 1: (t_pv1, pv, 512)}
                GQ = {0: (t_pq0, pq, 0), 1: (t_pq1, pq, 512)}

                def ghds(g):
                    blk, par = g // 2, g % 2
                    return [8 * blk + par + 2 * i for i in range(4)]

                def g_phase(g):
                    gp = g % 2
                    heads = ghds(g)
                    tG, psG, oG = GP[gp]
                    tGT, psGT, oGT = GV[gp]
                    tMT, psMT, oMT = GQ[gp]
                    P.emit_group("pe", [
                        (lambda i, h: lambda e: e.matmul(
                            psG[:, oG + i * 128:oG + (i + 1) * 128],
                            fmv(kt_fm, h), fmv(k_fm, h), start=True, stop=True))(i, h)
                        for i, h in enumerate(heads)],
                      reads=[t_ktfm, t_kfm], writes=[tG])
                    P.emit_group("pe", [
                        (lambda i, h: lambda e: e.matmul(
                            psGT[:, oGT + i * 128:oGT + (i + 1) * 128],
                            fmv(k_fm, h), fmv(kt_fm, h), start=True, stop=True))(i, h)
                        for i, h in enumerate(heads)],
                      reads=[t_ktfm, t_kfm], writes=[tGT])
                    P.emit_group("pe", [
                        (lambda i, h: lambda e: e.matmul(
                            psMT[:, oMT + i * 128:oMT + (i + 1) * 128],
                            fmv(k_fm, h), fmv(q_fm, h), start=True, stop=True))(i, h)
                        for i, h in enumerate(heads)],
                      reads=[t_qfm, t_kfm], writes=[tMT])
                    P.emit_group("pe", [
                        (lambda i, h, gp: lambda e: e.matmul(
                            p7[:, gp * 256 + i * 64:gp * 256 + (i + 1) * 64],
                            fmv(kt_fm, h), Sbv(h), start=True, stop=True))(i, h, gp)
                        for i, h in enumerate(heads)],
                      reads=[t_ktfm] + [t_Sb[h // 2] for h in heads], writes=[t_p7])
                    E("dve", lambda e, gp=gp: e.tensor_tensor(
                        P_s[:, gp * 1024:gp * 1024 + 512],
                        psG[:, oG:oG + 512], mlo, ALU.mult),
                      reads=[tG, t_masks], writes=[t_P[gp * 2 + 0]])
                    E("dve", lambda e, gp=gp: e.tensor_tensor(
                        PT_s[:, gp * 1024:gp * 1024 + 512],
                        psGT[:, oGT:oGT + 512], mup, ALU.mult),
                      reads=[tGT, t_masks], writes=[t_PT[gp * 2 + 0]])
                    E("dve", lambda e, gp=gp: e.tensor_tensor(
                        MT_s[:, gp * 512:(gp + 1) * 512],
                        psMT[:, oMT:oMT + 512], mui, ALU.mult),
                      reads=[tMT, t_masks], writes=[t_MT[gp]])
                    vbase = (8 * (g // 2) + (g % 2)) * 64
                    E("dve", lambda e, gp=gp, vbase=vbase: e.tensor_tensor(
                        tmp_s[:, gp * 512:gp * 512 + 256].rearrange("p (h d) -> p h d", d=64),
                        vt_tm[:, vbase:vbase + 448].rearrange("p (h d) -> p h d", d=64)[:, 0::2, :],
                        p7[:, gp * 256:gp * 256 + 256].rearrange("p (h d) -> p h d", d=64),
                        ALU.subtract),
                      reads=[t_vttm, t_p7], writes=[t_tmp[gp * 2 + 0]])

                def solve_level(g, lv):
                    gp = g % 2
                    lp = lv % 2
                    nlp = 1 - lp
                    tG, psG, oG = GP[gp]
                    tGT, psGT, oGT = GV[gp]
                    tAp, psAp, oAp = GQ[gp]
                    pc = (gp * 2 + lp) * 512
                    npc = (gp * 2 + nlp) * 512
                    tc = (gp * 2 + lp) * 256
                    ntc = (gp * 2 + nlp) * 256
                    P.emit_group("pe", [
                        (lambda i, pc, tc: lambda e: e.matmul(
                            psAp[:, oAp + i * 64:oAp + (i + 1) * 64],
                            PT_s[:, pc + i * 128:pc + (i + 1) * 128],
                            tmp_s[:, tc + i * 64:tc + (i + 1) * 64],
                            start=True, stop=True))(i, pc, tc)
                        for i in range(4)],
                      reads=[t_PT[gp * 2 + lp], t_tmp[gp * 2 + lp]], writes=[tAp])
                    E("dve", lambda e, tc=tc, ntc=ntc: e.tensor_tensor(
                        tmp_s[:, ntc:ntc + 256], tmp_s[:, tc:tc + 256],
                        psAp[:, oAp:oAp + 256], ALU.add),
                      reads=[t_tmp[gp * 2 + lp], tAp], writes=[t_tmp[gp * 2 + nlp]])
                    if lv < 6:
                        P.emit_group("pe", [
                            (lambda i, pc: lambda e: e.matmul(
                                psG[:, oG + i * 128:oG + (i + 1) * 128],
                                PT_s[:, pc + i * 128:pc + (i + 1) * 128],
                                P_s[:, pc + i * 128:pc + (i + 1) * 128],
                                start=True, stop=True))(i, pc)
                            for i in range(4)],
                          reads=[t_P[gp * 2 + lp], t_PT[gp * 2 + lp]], writes=[tG])
                        P.emit_group("pe", [
                            (lambda i, pc: lambda e: e.matmul(
                                psGT[:, oGT + i * 128:oGT + (i + 1) * 128],
                                P_s[:, pc + i * 128:pc + (i + 1) * 128],
                                PT_s[:, pc + i * 128:pc + (i + 1) * 128],
                                start=True, stop=True))(i, pc)
                            for i in range(4)],
                          reads=[t_P[gp * 2 + lp], t_PT[gp * 2 + lp]], writes=[tGT])
                        E("act", lambda e, npc=npc: e.activation(
                            P_s[:, npc:npc + 512], psG[:, oG:oG + 512], AF.Copy),
                          reads=[tG], writes=[t_P[gp * 2 + nlp]])
                        E("dve", lambda e, npc=npc: e.tensor_copy(
                            PT_s[:, npc:npc + 512], psGT[:, oGT:oGT + 512]),
                          reads=[tGT], writes=[t_PT[gp * 2 + nlp]])

                def oi_phase(g):
                    gp = g % 2
                    heads = ghds(g)
                    uc = (gp * 2 + 1) * 256
                    tU = t_tmp[gp * 2 + 1]
                    tAp, psAp, oAp = GQ[gp]
                    ofns = []
                    for i, h in enumerate(heads):
                        p = h // 2
                        off = (h % 2) * 64
                        col = (p % 4) * 128
                        ofns.append((lambda i, off, col, gp, uc: lambda e: e.matmul(
                            p7[off:off + 64, col:col + 128],
                            tmp_s[:, uc + i * 64:uc + (i + 1) * 64],
                            MT_s[:, gp * 512 + i * 128:gp * 512 + (i + 1) * 128],
                            start=True, stop=False))(i, off, col, gp, uc))
                        ofns.append((lambda h, off, col: lambda e: e.matmul(
                            p7[off:off + 64, col:col + 128],
                            Sbv(h), fmv(q_fm, h), start=False, stop=True))(h, off, col))
                    P.emit_group("pe", ofns,
                      reads=[tU, t_MT[gp], t_qfm] + [t_Sb[h // 2] for h in heads],
                      writes=[t_p7])
                    P.emit_group("pe", [
                        (lambda h, i, uc: lambda e: e.matmul(
                            psAp[(h % 2) * 64:(h % 2) * 64 + 64,
                                 oAp + 256 + i * 64:oAp + 256 + (i + 1) * 64],
                            k_tm[:, h * 64:(h + 1) * 64],
                            tmp_s[:, uc + i * 64:uc + (i + 1) * 64],
                            start=True, stop=True))(h, i, uc)
                        for i, h in enumerate(heads)],
                      reads=[t_ktm, tU], writes=[tAp])
                    off = (g % 2) * 64
                    pr0 = (g // 2) * 4
                    prs = [pr0 + j for j in range(4)]
                    E("dve", lambda e, off=off, pr0=pr0: e.tensor_tensor(
                        S_f[off:off + 64, pr0 * 64:(pr0 + 4) * 64],
                        S_f[off:off + 64, pr0 * 64:(pr0 + 4) * 64],
                        psAp[off:off + 64, oAp + 256:oAp + 512], ALU.add),
                      reads=[tAp] + [t_Sf[p2] for p2 in prs],
                      writes=[t_Sf[p2] for p2 in prs])
                    E("act", lambda e, off=off, pr0=pr0: e.activation(
                        S_b[off:off + 64, pr0 * 64:(pr0 + 4) * 64],
                        S_f[off:off + 64, pr0 * 64:(pr0 + 4) * 64], AF.Copy),
                      reads=[t_Sf[p2] for p2 in prs],
                      writes=[t_Sb[p2] for p2 in prs])

                def front(t):
                    bi = t % 2
                    E("dve", lambda e, bi=bi: e.tensor_reduce(msum, x_in[bi][:, :], AXX, ALU.add),
                      reads=[t_xin[bi]], writes=[t_sm])
                    E("act", lambda e, bi=bi: e.activation(sqs[:, :], x_in[bi][:, :], AF.Square,
                                                           accum_out=sqsum),
                      reads=[t_xin[bi]], writes=[t_sqs, t_sm])
                    E("dve", lambda e: e.tensor_scalar(negmean, msum, -1.0 / DM, None, ALU.mult),
                      reads=[t_sm], writes=[t_sm])
                    E("dve", lambda e: e.tensor_tensor(m2, negmean, negmean, ALU.mult),
                      reads=[t_sm], writes=[t_sm])
                    E("dve", lambda e: e.scalar_tensor_tensor(veps, sqsum, 1.0 / DM, m2,
                                                              ALU.mult, ALU.subtract),
                      reads=[t_sm], writes=[t_sm])
                    E("act", lambda e: e.activation(lnv, veps, AF.Ln, bias=lneps[:, :]),
                      reads=[t_sm, t_lneps], writes=[t_sm])
                    E("act", lambda e: e.activation(rstd, lnv, AF.Exp, scale=-0.5),
                      reads=[t_sm], writes=[t_sm])
                    E("dve", lambda e, bi=bi: e.tensor_scalar(xhat[:, :], x_in[bi][:, :],
                                                              negmean, rstd, ALU.add, ALU.mult),
                      reads=[t_xin[bi], t_sm], writes=[t_xhat])
                    P.emit_group("pe", [
                        (lambda cc: lambda e: e.transpose(
                            pT[:, cc * 128:(cc + 1) * 128],
                            xhat[:, cc * 128:(cc + 1) * 128], ident[:, :]))(cc)
                        for cc in range(8)],
                      reads=[t_xhat, t_ident], writes=[t_pT])
                    E("act", lambda e: e.activation(xhfm2[t % 2][:, :], pT[:, :], AF.Copy),
                      reads=[t_pT], writes=[t_xhfm2[t % 2]])

                t00 = 0 if not skip_p1 else n_tiles
                if t00 < n_tiles:
                    front(t00)
                for t in range(t00, n_tiles):
                    bi = t % 2
                    r0 = t * 128
                    xhfm = xhfm2[t % 2]
                    t_xhfm = t_xhfm2[t % 2]
                    if t + 1 < n_tiles:
                        nb = (t + 1) % 2
                        nr = (t + 1) * 128
                        E("sync", lambda e, nb=nb, nr=nr: e.dma_start(
                            out=x_in[nb][:, :], in_=x_d[nr:nr + 128, :]),
                          writes=[t_xin[nb]])
                    if t >= getattr(Prog, 'WGT_T0', 2) and wgT_calls and not getattr(Prog, 'NO_FFNPREP', False):
                        ic = wgT_calls.pop(0)
                        E("sync", lambda e, ic=ic: e.dma_start(
                            out=wgT[:, ic * HID:(ic + 1) * HID],
                            in_=wgb_d[:, ic * 128:(ic + 1) * 128], transpose=True),
                          reads=[t_wgb], writes=[t_wgT])

                    def proj(dst_toks, w_s, t_w, psum, width):
                        if width == 1024:
                            for oc in range(2):
                                P.emit_group("pe", [
                                    (lambda oc, ic, w_s, psum: lambda e: e.matmul(
                                        psum[:, oc * 512:(oc + 1) * 512],
                                        xhfm[:, ic * 128:(ic + 1) * 128],
                                        w_s[:, ic * 1024 + oc * 512: ic * 1024 + (oc + 1) * 512],
                                        start=(ic == 0), stop=(ic == 7)))(oc, ic, w_s, psum)
                                    for ic in range(8)],
                                  reads=[t_xhfm, t_w], writes=[dst_toks[oc]])
                        else:
                            P.emit_group("pe", [
                                (lambda ic, w_s, psum: lambda e: e.matmul(
                                    psum[:, 0:H],
                                    xhfm[:, ic * 128:(ic + 1) * 128],
                                    w_s[:, ic * H:(ic + 1) * H],
                                    start=(ic == 0), stop=(ic == 7)))(ic, w_s, psum)
                                for ic in range(8)],
                              reads=[t_xhfm, t_w], writes=[dst_toks[0]])

                    proj([t_pk0, t_pk1], wk_s, t_wk, pk, 1024)
                    proj([t_p7], wb_s, t_wb, p7, H)
                    proj([t_pv0, t_pv1], wv_s, t_wv, pv, 1024)
                    proj([t_pq0, t_pq1], wq_s, t_wq, pq, 1024)

                    E("dve", lambda e: e.tensor_tensor(blin, p7[:, 0:H], bb_bc[:, :], ALU.add),
                      reads=[t_p7, t_bb], writes=[t_ks])
                    E("act", lambda e: e.activation(ksq2, blin, AF.Exp, scale=-1.0),
                      reads=[t_ks], writes=[t_ks])
                    E("dve", lambda e: e.tensor_scalar(rr, ksq2, 1.0, None, ALU.add),
                      reads=[t_ks], writes=[t_ks])
                    E("dve", lambda e: e.reciprocal(beta_v, rr),
                      reads=[t_ks], writes=[t_sb2])

                    E("act", lambda e: e.activation(sqs[:, 0:512], pk[:, 0:512], AF.Square),
                      reads=[t_pk0], writes=[t_sqs])
                    E("act", lambda e: e.activation(sqs[:, 512:1024], pk[:, 512:1024], AF.Square),
                      reads=[t_pk1], writes=[t_sqs])
                    E("dve", lambda e: e.tensor_reduce(
                        ksq[:, :].unsqueeze(2),
                        sqs[:, :].rearrange("p (h d) -> p h d", d=DH), AXX, ALU.add),
                      reads=[t_sqs], writes=[t_ks])
                    E("dve", lambda e: e.tensor_scalar(ksq2, ksq, 1e-24, None, ALU.max),
                      reads=[t_ks], writes=[t_ks])
                    E("act", lambda e: e.activation(blin, ksq2, AF.Ln),
                      reads=[t_ks], writes=[t_ks])
                    E("act", lambda e: e.activation(rr, blin, AF.Exp, scale=-0.5),
                      reads=[t_ks], writes=[t_ks])
                    E("dve", lambda e: e.tensor_tensor(s2c, beta_v, rr, ALU.mult),
                      reads=[t_sb2, t_ks], writes=[t_sb2])
                    E("dve", lambda e: e.tensor_tensor(s1c, s2c, rr, ALU.mult),
                      reads=[t_sb2, t_ks], writes=[t_sb2])

                    E("act", lambda e: e.activation(k_tm[:, :], pk[:, :], AF.Copy),
                      reads=[t_pk0, t_pk1], writes=[t_ktm])
                    E("dve", lambda e: e.tensor_tensor(
                        ktil[:, :].rearrange("p (h d) -> p h d", d=DH),
                        pk[:, :].rearrange("p (h d) -> p h d", d=DH),
                        s1c.unsqueeze(2).broadcast_to([128, H, DH]), ALU.mult),
                      reads=[t_pk0, t_pk1, t_sb2], writes=[t_ktil])
                    E("dve", lambda e: e.tensor_tensor(
                        vt_tm[:, :].rearrange("p (h d) -> p h d", d=DH),
                        pv[:, :].rearrange("p (h d) -> p h d", d=DH),
                        s2c.unsqueeze(2).broadcast_to([128, H, DH]), ALU.mult),
                      reads=[t_pv0, t_pv1, t_sb2], writes=[t_vttm])
                    E("act", lambda e: e.activation(q_tm[:, :], pq[:, :], AF.Copy),
                      reads=[t_pq0, t_pq1], writes=[t_qtm])

                    for (src, tsrc, dst, tdst) in ((k_tm, t_ktm, k_fm, t_kfm),
                                                   (ktil, t_ktil, kt_fm, t_ktfm),
                                                   (q_tm, t_qtm, q_fm, t_qfm)):
                        P.emit_group("pe", [
                            (lambda p, src: lambda e: e.transpose(
                                pT[:, p * 128:(p + 1) * 128],
                                src[:, p * 128:(p + 1) * 128], ident[:, :]))(p, src)
                            for p in range(8)],
                          reads=[tsrc, t_ident], writes=[t_pT])
                        E("act", lambda e, dst=dst: e.activation(dst[:, :], pT[:, :], AF.Copy),
                          reads=[t_pT], writes=[tdst])

                    for gpair in ((0, 1), (2, 3)):
                        for g in gpair:
                            g_phase(g)
                        for lv in range(7):
                            for g in gpair:
                                solve_level(g, lv)
                        if gpair[0] == 0 and t + 1 < n_tiles:
                            front(t + 1)
                        for g in gpair:
                            oi_phase(g)
                        lo = 0 if gpair[0] == 0 else 512
                        E("act", lambda e, lo=lo: e.activation(
                            o_fm[:, lo:lo + 512], p7[:, 0:512], AF.Copy),
                          reads=[t_p7], writes=[t_ofm])

                    for oc in range(2):
                        tdst = [t_pv0, t_pv1][oc]
                        P.emit_group("pe", [
                            (lambda oc, p: lambda e: e.matmul(
                                pv[:, oc * 512:(oc + 1) * 512],
                                o_fm[:, p * 128:(p + 1) * 128],
                                wo_s[:, p * 1024 + oc * 512:p * 1024 + (oc + 1) * 512],
                                start=(p == 0), stop=(p == 7)))(oc, p)
                            for p in range(8)],
                          reads=[t_ofm, t_wo], writes=[tdst])
                    E("dve", lambda e, bi=bi: e.tensor_tensor(
                        x1_s[bi][:, 0:512], x_in[bi][:, 0:512], pv[:, 0:512], ALU.add),
                      reads=[t_xin[bi], t_pv0], writes=[t_x1sb[bi]])
                    E("dve", lambda e, bi=bi: e.tensor_tensor(
                        x1_s[bi][:, 512:1024], x_in[bi][:, 512:1024], pv[:, 512:1024], ALU.add),
                      reads=[t_xin[bi], t_pv1], writes=[t_x1sb[bi]])

                    E("dve", lambda e, bi=bi: e.tensor_reduce(m2sum, x1_s[bi][:, :], AXX, ALU.add),
                      reads=[t_x1sb[bi]], writes=[t_sm2])
                    E("act", lambda e, bi=bi: e.activation(sqs[:, :], x1_s[bi][:, :], AF.Square,
                                                           accum_out=sq2sum),
                      reads=[t_x1sb[bi]], writes=[t_sqs, t_sm2])
                    E("dve", lambda e, t=t: e.tensor_scalar(
                        negm2[:, t:t + 1], m2sum, -1.0 / DM, None, ALU.mult),
                      reads=[t_sm2], writes=[t_negm2])
                    E("dve", lambda e, t=t: e.tensor_tensor(
                        m22, negm2[:, t:t + 1], negm2[:, t:t + 1], ALU.mult),
                      reads=[t_negm2], writes=[t_sm2])
                    E("dve", lambda e: e.scalar_tensor_tensor(veps2, sq2sum, 1.0 / DM, m22,
                                                              ALU.mult, ALU.subtract),
                      reads=[t_sm2], writes=[t_sm2])
                    E("act", lambda e: e.activation(lnv2, veps2, AF.Ln, bias=lneps[:, :]),
                      reads=[t_sm2, t_lneps], writes=[t_sm2])
                    E("act", lambda e, t=t: e.activation(
                        rstd2[:, t:t + 1], lnv2, AF.Exp, scale=-0.5),
                      reads=[t_sm2], writes=[t_rstd2])

                    E("pool", lambda e, bi=bi, r0=r0: e.dma_start(
                        out=x1_d[r0:r0 + 128, :], in_=x1_s[bi][:, :]),
                      reads=[t_x1sb[bi]], writes=[t_x1s[t]])
                    if dbg:
                        E("pool", lambda e, bi=bi, r0=r0: e.dma_start(
                            out=dbg_d["dbg_x1"][r0:r0 + 128, :], in_=x1_s[bi][:, :]),
                          reads=[t_x1sb[bi]])

                    if (t + 1) % tps == 0 and t + 1 < n_tiles:
                        E("dve", lambda e: e.memset(S_f[:, :], 0.0), writes=t_Sf)
                        E("dve", lambda e: e.memset(S_b[:, :], 0.0), writes=t_Sb)

                while wgT_calls and not getattr(Prog, "NO_FFNPREP", False):
                    ic = wgT_calls.pop(0)
                    E("sync", lambda e, ic=ic: e.dma_start(
                        out=wgT[:, ic * HID:(ic + 1) * HID],
                        in_=wgb_d[:, ic * 128:(ic + 1) * 128], transpose=True),
                      reads=[t_wgb], writes=[t_wgT])

                if dbg:
                    E("pool", lambda e: e.dma_start(out=dbg_d["dbg_x1d"][:, :], in_=x1_d[:, :]),
                      reads=list(t_x1s))
                    E("pool", lambda e: e.dma_start(out=dbg_d["dbg_st"][:, 0:n_tiles], in_=negm2[:, :]),
                      reads=[t_negm2])
                    E("pool", lambda e: e.dma_start(out=dbg_d["dbg_st"][:, n_tiles:2 * n_tiles], in_=rstd2[:, :]),
                      reads=[t_rstd2])

                P.barrier()

                if skip_p2:
                    P.finish()
                    return
                t_wuT = Tok(); t_wdT = Tok()
                t_x1i = [Tok(), Tok()]
                t_x1r = [Tok(), Tok()]
                t_xh2 = Tok(); t_xh2fm = Tok()
                t_hfm = Tok()
                t_gs = [Tok(), Tok()]
                t_y = [Tok(), Tok()]

                for ic in range(8):
                    E("sync", lambda e, ic=ic: e.dma_start(
                        out=wuT[:, ic * HID:(ic + 1) * HID],
                        in_=wub_d[:, ic * 128:(ic + 1) * 128], transpose=True),
                      reads=[t_wub], writes=[t_wuT])
                for hc in range(24):
                    E("sync", lambda e, hc=hc: e.dma_start(
                        out=wdT[:, hc * 1024:(hc + 1) * 1024],
                        in_=wdb_d[:, hc * 128:(hc + 1) * 128], transpose=True),
                      reads=[t_wdb], writes=[t_wdT])

                n_grp = n_tiles // 4
                for G in range(n_grp):
                    t0 = G * 4
                    for sub in range(4):
                        tt = t0 + sub
                        bi = sub % 2
                        E("sync", lambda e, tt=tt, bi=bi: e.dma_start(
                            out=x1_in[bi][:, :], in_=x1_d[tt * 128:(tt + 1) * 128, :]),
                          reads=[t_x1s[tt]], writes=[t_x1i[bi]])
                        E("dve", lambda e, tt=tt, bi=bi: e.tensor_scalar(
                            xh2[:, :], x1_in[bi][:, :],
                            negm2[:, tt:tt + 1], rstd2[:, tt:tt + 1], ALU.add, ALU.mult),
                          reads=[t_x1i[bi], t_negm2, t_rstd2], writes=[t_xh2])
                        P.emit_group("pe", [
                            (lambda cc: lambda e: e.transpose(
                                pT[:, cc * 128:(cc + 1) * 128],
                                xh2[:, cc * 128:(cc + 1) * 128], ident[:, :]))(cc)
                            for cc in range(8)],
                          reads=[t_xh2, t_ident], writes=[t_pT])
                        E("act", lambda e, sub=sub: e.activation(
                            xh2fm[:, :].rearrange("p (a b) -> p a b", b=512)[:, :, sub * 128:(sub + 1) * 128],
                            pT[:, :].rearrange("p (a b) -> p a b", b=128), AF.Copy),
                          reads=[t_pT], writes=[t_xh2fm])

                    for hc in range(24):
                        par = hc % 2
                        tg = [t_pk0, t_pk1][par]
                        tu = [t_pv0, t_pv1][par]
                        og = par * 512
                        P.emit_group("pe", [
                            (lambda hc, ic, og: lambda e: e.matmul(
                                pk[:, og:og + 512],
                                wgT[:, ic * HID + hc * 128:ic * HID + (hc + 1) * 128],
                                xh2fm[:, ic * 512:(ic + 1) * 512],
                                start=(ic == 0), stop=(ic == 7)))(hc, ic, og)
                            for ic in range(8)],
                          reads=[t_wgT, t_xh2fm], writes=[tg])
                        P.emit_group("pe", [
                            (lambda hc, ic, og: lambda e: e.matmul(
                                pv[:, og:og + 512],
                                wuT[:, ic * HID + hc * 128:ic * HID + (hc + 1) * 128],
                                xh2fm[:, ic * 512:(ic + 1) * 512],
                                start=(ic == 0), stop=(ic == 7)))(hc, ic, og)
                            for ic in range(8)],
                          reads=[t_wuT, t_xh2fm], writes=[tu])
                        E("act", lambda e, og=og, par=par: e.activation(
                            g_s[par][:, :], pk[:, og:og + 512], AF.Silu, scale=SC),
                          reads=[tg], writes=[t_gs[par]])
                        E("dve", lambda e, og=og, par=par, hc=hc: e.tensor_tensor(
                            h_fm[:, hc * 512:(hc + 1) * 512],
                            g_s[par][:, :], pv[:, og:og + 512], ALU.mult),
                          reads=[t_gs[par], tu], writes=[t_hfm])

                    for sub in range(4):
                        tt = t0 + sub
                        bi = sub % 2
                        E("sync", lambda e, tt=tt, bi=bi: e.dma_start(
                            out=x1_r[bi][:, :], in_=x1_d[tt * 128:(tt + 1) * 128, :]),
                          reads=[t_x1s[tt]], writes=[t_x1r[bi]])
                        for oc in range(2):
                            tdst = [t_pq0, t_pq1][oc]
                            P.emit_group("pe", [
                                (lambda sub, oc, hc: lambda e: e.matmul(
                                    pq[:, oc * 512:(oc + 1) * 512],
                                    h_fm[:, hc * 512 + sub * 128:hc * 512 + (sub + 1) * 128],
                                    wdT[:, hc * 1024 + oc * 512:hc * 1024 + (oc + 1) * 512],
                                    start=(hc == 0), stop=(hc == 23)))(sub, oc, hc)
                                for hc in range(24)],
                              reads=[t_hfm, t_wdT], writes=[tdst])
                        E("dve", lambda e, bi=bi: e.scalar_tensor_tensor(
                            y_sb[bi][:, 0:512], pq[:, 0:512], SC * SC,
                            x1_r[bi][:, 0:512], ALU.mult, ALU.add),
                          reads=[t_pq0, t_x1r[bi]], writes=[t_y[bi]])
                        E("dve", lambda e, bi=bi: e.scalar_tensor_tensor(
                            y_sb[bi][:, 512:1024], pq[:, 512:1024], SC * SC,
                            x1_r[bi][:, 512:1024], ALU.mult, ALU.add),
                          reads=[t_pq1, t_x1r[bi]], writes=[t_y[bi]])
                        E("pool", lambda e, tt=tt, bi=bi: e.dma_start(
                            out=out_d[tt * 128:(tt + 1) * 128, :], in_=y_sb[bi][:, :]),
                          reads=[t_y[bi]], writes=[])

                P.finish()

            P.run(body)
    return nc


def _host_consts():
    ident = np.eye(128, dtype=BFD)
    ml = np.tril(-np.ones((128, 128), dtype=np.float32), -1).astype(BFD)
    mu = np.triu(-np.ones((128, 128), dtype=np.float32), 1).astype(BFD)
    mi = np.triu(np.ones((128, 128), dtype=np.float32), 0).astype(BFD)
    masks = np.concatenate([np.tile(m, (1, 4)) for m in (ml, mu, mi)], axis=1)
    return ident, np.ascontiguousarray(masks)


def kernel(L0_attn_x, L0_attn_Wq, L0_attn_Wk, L0_attn_Wv, L0_attn_Wbeta,
           L0_attn_b_beta, L0_attn_Wout, L0_ffn_w_gate, L0_ffn_w_up,
           L0_ffn_w_down):
    x = np.ascontiguousarray(np.asarray(L0_attn_x, dtype=np.float32))
    B, T, _ = x.shape
    ident, masks = _host_consts()
    bb_bc = np.ascontiguousarray(
        np.broadcast_to(np.asarray(L0_attn_b_beta, np.float32)[None, :], (128, H)))
    common = {
        "wq": np.ascontiguousarray(np.asarray(L0_attn_Wq, np.float32)),
        "wk": np.ascontiguousarray(np.asarray(L0_attn_Wk, np.float32)),
        "wv": np.ascontiguousarray(np.asarray(L0_attn_Wv, np.float32)),
        "wb": np.ascontiguousarray(np.asarray(L0_attn_Wbeta, np.float32)),
        "wo": np.ascontiguousarray(np.asarray(L0_attn_Wout, np.float32)),
        "wg": np.ascontiguousarray(np.asarray(L0_ffn_w_gate, np.int32)),
        "wu": np.ascontiguousarray(np.asarray(L0_ffn_w_up, np.int32)),
        "wd": np.ascontiguousarray(np.asarray(L0_ffn_w_down, np.int32)),
        "bbc": bb_bc, "cident": ident, "cmasks": masks,
    }
    spc = B // NCORES
    in_maps = []
    for c in range(NCORES):
        m = dict(common)
        m["x"] = np.ascontiguousarray(
            x[c * spc:(c + 1) * spc].reshape(spc * T, DM))
        in_maps.append(m)
    nc = build_nc(n_seq=spc, tps=T // CT)
    res = run_bass_kernel_spmd(nc, in_maps, core_ids=list(range(NCORES)))
    out = np.stack([res.results[c]["out"].reshape(spc, T, DM)
                    for c in range(NCORES)], axis=0)
    return np.ascontiguousarray(out.reshape(B, T, DM))


# version 10 (final)
# speedup vs baseline: 1.1894x; 1.1894x over previous
import numpy as np
import ml_dtypes
from contextlib import ExitStack
import concourse.bass as bass
import concourse.mybir as mybir
from concourse.bass_utils import run_bass_kernel_spmd

F32 = mybir.dt.float32
BF16 = mybir.dt.bfloat16
I32 = mybir.dt.int32
AF = mybir.ActivationFunctionType
ALU = mybir.AluOpType
AXX = mybir.AxisListType.X
BFD = ml_dtypes.bfloat16

DM = 1024
H = 16
DH = 64
HID = 3072
CT = 128
SC = 1.0 / 127.0
NCORES = 8


class Tok:
    __slots__ = ("last_w", "readers", "name")

    def __init__(self, name=""):
        self.last_w = None
        self.readers = []
        self.name = name


class Prog:
    DMA_ENGS = ("sync", "pool")
    SELF_SKIP = ("sync", "pool")
    ALL_MS = False
    DENSE_ENGS = ("dve", "act")

    NLANES = 8

    def __init__(self, nc):
        self.nc = nc
        self.engs = {
            "pe": nc.tensor, "dve": nc.vector, "act": nc.scalar,
            "sync": nc.sync, "pool": nc.gpsimd,
        }
        self.sem = {e: nc.alloc_semaphore(name=f"S_{e}")
                    for e in self.engs if e not in self.DMA_ENGS}
        self.lanes = {e: [nc.alloc_semaphore(name=f"S_{e}{i}")
                          for i in range(self.NLANES)] for e in self.DMA_ENGS}
        self.mode = "record"

    def _reset(self):
        self.idx = {e: 0 for e in self.engs}

    def emit(self, eng, fn, reads=(), writes=(), inc=None):
        deps = {}
        for tk in reads:
            if tk.last_w is not None:
                e2, v = tk.last_w
                if deps.get(e2, 0) < v:
                    deps[e2] = v
        for tk in writes:
            if tk.last_w is not None:
                e2, v = tk.last_w
                if deps.get(e2, 0) < v:
                    deps[e2] = v
            for (e2, v) in tk.readers:
                if deps.get(e2, 0) < v:
                    deps[e2] = v
        if self.mode == "record":
            self.sig.append((eng, tuple(sorted(deps.items()))))
            for e2, v in deps.items():
                if e2 == eng and eng in self.SELF_SKIP:
                    continue
                self.ms[e2].add(v)
        else:
            exp = self.sig[self.signo]
            got = (eng, tuple(sorted(deps.items())))
            assert exp == got, f"record/replay divergence at op {self.signo}: {exp} vs {got}"
            self.signo += 1
            self._emit_waits(eng, deps)
        self.idx[eng] += 1
        me = (eng, self.idx[eng])
        if self.mode == "replay":
            self._emit_op(eng, fn, self.idx[eng])
        for tk in reads:
            tk.readers.append(me)
        for tk in writes:
            tk.last_w = me
            tk.readers = []

    NOGRP = False

    def _emit_waits(self, eng, deps):
        for e2 in sorted(deps):
            if e2 == eng and eng in self.SELF_SKIP:
                continue
            v = deps[e2]
            if e2 in self.DMA_ENGS:
                r = (v - 1) % self.NLANES
                for lane in range(self.NLANES):
                    o = v - ((r - lane) % self.NLANES)
                    if o < 1:
                        continue
                    val = 16 * ((o - 1) // self.NLANES + 1)
                    key = (e2, lane)
                    if val > self.waited[eng].get(key, 0):
                        self.engs[eng].wait_ge(self.lanes[e2][lane], val)
                        self.waited[eng][key] = val
            else:
                if self.msno[e2][v] > self.waited[eng].get(e2, 0):
                    self.engs[eng].wait_ge(self.sem[e2], self.msno[e2][v])
                    self.waited[eng][e2] = self.msno[e2][v]

    def _emit_op(self, eng, fn, ordinal):
        if eng in self.DMA_ENGS:
            lane = (ordinal - 1) % self.NLANES
            use = (ordinal - 1) // self.NLANES + 1
            if use >= 2:
                key = (eng, lane)
                val = 16 * (use - 1)
                if val > self.waited[eng].get(key, 0):
                    self.engs[eng].wait_ge(self.lanes[eng][lane], val)
                    self.waited[eng][key] = val
            ins = fn(self.engs[eng])
            ins.then_inc(self.lanes[eng][lane], 16)
        else:
            ins = fn(self.engs[eng])
            if ordinal in self.ms[eng]:
                ins.then_inc(self.sem[eng], self.unit[eng])
        return ins

    def emit_group(self, eng, fns, reads=(), writes=()):
        if self.NOGRP:
            for fn in fns:
                self.emit(eng, fn, reads=reads, writes=writes)
            return
        deps = {}
        for tk in list(reads) + list(writes):
            if tk.last_w is not None:
                e2, v = tk.last_w
                if deps.get(e2, 0) < v:
                    deps[e2] = v
        for tk in writes:
            for (e2, v) in tk.readers:
                if deps.get(e2, 0) < v:
                    deps[e2] = v
        if self.mode == "record":
            self.sig.append((eng, tuple(sorted(deps.items())), len(fns)))
            for e2, v in deps.items():
                if e2 == eng and eng in self.SELF_SKIP:
                    continue
                self.ms[e2].add(v)
        else:
            exp = self.sig[self.signo]
            got = (eng, tuple(sorted(deps.items())), len(fns))
            assert exp == got, f"record/replay divergence at {self.signo}: {exp} vs {got}"
            self.signo += 1
            self._emit_waits(eng, deps)
        for fn in fns:
            self.idx[eng] += 1
            if self.mode == "replay":
                self._emit_op(eng, fn, self.idx[eng])
        me = (eng, self.idx[eng])
        for tk in reads:
            tk.readers.append(me)
        for tk in writes:
            tk.last_w = me
            tk.readers = []

    def _wait_spec(self, eng, deps):
        if self.mode == "record":
            for e2, v in deps.items():
                self.ms[e2].add(v)
        else:
            for e2 in sorted(deps):
                v = deps[e2]
                if e2 in self.DMA_ENGS:
                    r = (v - 1) % self.NLANES
                    for lane in range(self.NLANES):
                        o = v - ((r - lane) % self.NLANES)
                        if o < 1:
                            continue
                        val = 16 * ((o - 1) // self.NLANES + 1)
                        key = (e2, lane)
                        if val > self.waited[eng].get(key, 0):
                            self.engs[eng].wait_ge(self.lanes[e2][lane], val)
                            self.waited[eng][key] = val
                else:
                    if self.msno[e2][v] > self.waited[eng].get(e2, 0):
                        self.engs[eng].wait_ge(self.sem[e2], self.msno[e2][v])
                        self.waited[eng][e2] = self.msno[e2][v]

    def barrier(self):
        for e in self.engs:
            deps = {e2: self.idx[e2] for e2 in self.engs
                    if e2 != e and self.idx[e2] > 0}
            self._wait_spec(e, deps)

    def finish(self):
        deps = {e2: self.idx[e2] for e2 in self.engs
                if e2 != "sync" and self.idx[e2] > 0}
        self._wait_spec("sync", deps)
        if self.idx["sync"] > 0:
            self._wait_spec("pool", {"sync": self.idx["sync"]})

    def run(self, body):
        self.mode = "record"
        self._reset()
        self.ms = {e: set() for e in self.engs}
        self.sig = []
        self.signo = 0
        body()
        for e in self.engs:
            if self.idx[e] > 0:
                self.ms[e].add(self.idx[e])
        for e in self.DMA_ENGS:
            self.ms[e] = set(range(1, self.idx[e] + 1))
        if self.ALL_MS:
            for e in self.engs:
                self.ms[e] = set(range(1, self.idx[e] + 1))
        for e in self.DENSE_ENGS:
            self.ms[e] = set(range(1, self.idx[e] + 1))
        self.msno = {}
        for e in self.engs:
            cnt = 0
            arr = [0] * (self.idx[e] + 1)
            for i in range(1, self.idx[e] + 1):
                if i in self.ms[e]:
                    cnt += 1
                arr[i] = cnt
            self.msno[e] = arr
        self.unit = {e: 1 for e in self.engs}
        self.waited = {e: {} for e in self.engs}
        self.mode = "replay"
        self._reset()
        body()


def build_nc(n_seq=2, tps=32, dbg=False, skip_p2=False, skip_p1=False):
    n_tiles = n_seq * tps
    NR = n_tiles * CT

    nc = bass.Bass("TRN2", target_bir_lowering=False, debug=False)
    dbg_d = {}
    if dbg:
        for nm, sh, dt in (("dbg_x1", [NR, 1024], F32), ("dbg_st", [128, 2 * n_tiles], F32), ("dbg_x1d", [NR, 1024], F32)):
            dbg_d[nm] = nc.dram_tensor(nm, sh, dt, kind="ExternalOutput").ap()

    x_d = nc.dram_tensor("x", [NR, DM], F32, kind="ExternalInput").ap()
    wq_d = nc.dram_tensor("wq", [DM, DM], F32, kind="ExternalInput").ap()
    wk_d = nc.dram_tensor("wk", [DM, DM], F32, kind="ExternalInput").ap()
    wv_d = nc.dram_tensor("wv", [DM, DM], F32, kind="ExternalInput").ap()
    wb_d = nc.dram_tensor("wb", [DM, H], F32, kind="ExternalInput").ap()
    wo_d = nc.dram_tensor("wo", [DM, DM], F32, kind="ExternalInput").ap()
    wg_d = nc.dram_tensor("wg", [HID, DM], I32, kind="ExternalInput").ap()
    wu_d = nc.dram_tensor("wu", [HID, DM], I32, kind="ExternalInput").ap()
    wd_d = nc.dram_tensor("wd", [DM, HID], I32, kind="ExternalInput").ap()
    bb_d = nc.dram_tensor("bbc", [128, H], F32, kind="ExternalInput").ap()
    id_d = nc.dram_tensor("cident", [128, 128], BF16, kind="ExternalInput").ap()
    mk_d = nc.dram_tensor("cmasks", [128, 3 * 512], BF16, kind="ExternalInput").ap()
    out_d = nc.dram_tensor("out", [NR, DM], F32, kind="ExternalOutput").ap()
    x1_d = nc.dram_tensor("x1scr", [NR, DM], F32, kind="Internal").ap()
    wgb_d = nc.dram_tensor("wg_bf", [HID, DM], BF16, kind="Internal").ap()
    wub_d = nc.dram_tensor("wu_bf", [HID, DM], BF16, kind="Internal").ap()
    wdb_d = nc.dram_tensor("wd_bf", [DM, HID], BF16, kind="Internal").ap()

    st = ExitStack()
    with st:
        ident = st.enter_context(nc.sbuf_tensor("ident", [128, 128], BF16))
        bb_bc = st.enter_context(nc.sbuf_tensor("bb_bc", [128, H], F32))
        wgT = st.enter_context(nc.sbuf_tensor("wgT", [128, 8 * HID], BF16))
        negm2 = st.enter_context(nc.sbuf_tensor("negm2", [128, n_tiles], F32))
        rstd2 = st.enter_context(nc.sbuf_tensor("rstd2", [128, n_tiles], F32))
        lneps = st.enter_context(nc.sbuf_tensor("lneps", [128, 1], F32))

        pk = st.enter_context(nc.psum_tensor("pk", [128, 1024], F32))
        pv = st.enter_context(nc.psum_tensor("pv", [128, 1024], F32))
        pq = st.enter_context(nc.psum_tensor("pq", [128, 1024], F32))
        p7 = st.enter_context(nc.psum_tensor("p7", [128, 512], F32))
        pT = st.enter_context(nc.psum_tensor("pT", [128, 1024], BF16))

        st1 = ExitStack()
        masks = st1.enter_context(nc.sbuf_tensor("masks", [128, 3 * 512], BF16))
        mlo = masks[:, 0:512]
        mup = masks[:, 512:1024]
        mui = masks[:, 1024:1536]
        wq_s = st1.enter_context(nc.sbuf_tensor("wq_s", [128, 8 * 1024], BF16))
        wk_s = st1.enter_context(nc.sbuf_tensor("wk_s", [128, 8 * 1024], BF16))
        wv_s = st1.enter_context(nc.sbuf_tensor("wv_s", [128, 8 * 1024], BF16))
        wo_s = st1.enter_context(nc.sbuf_tensor("wo_s", [128, 8 * 1024], BF16))
        wb_s = st1.enter_context(nc.sbuf_tensor("wb_s", [128, 8 * H], BF16))
        x_in = [st1.enter_context(nc.sbuf_tensor(f"x_in{i}", [128, 1024], F32)) for i in range(2)]
        x1_s = [st1.enter_context(nc.sbuf_tensor(f"x1_s{i}", [128, 1024], F32)) for i in range(2)]
        xhat = st1.enter_context(nc.sbuf_tensor("xhat", [128, 1024], BF16))
        xhfm2 = [st1.enter_context(nc.sbuf_tensor(f"xhfm{i}", [128, 1024], BF16)) for i in range(2)]
        q_tm = st1.enter_context(nc.sbuf_tensor("q_tm", [128, 1024], BF16))
        k_tm = st1.enter_context(nc.sbuf_tensor("k_tm", [128, 1024], BF16))
        ktil = st1.enter_context(nc.sbuf_tensor("ktil", [128, 1024], BF16))
        vt_tm = st1.enter_context(nc.sbuf_tensor("vt_tm", [128, 1024], BF16))
        q_fm = st1.enter_context(nc.sbuf_tensor("q_fm", [128, 1024], BF16))
        k_fm = st1.enter_context(nc.sbuf_tensor("k_fm", [128, 1024], BF16))
        kt_fm = st1.enter_context(nc.sbuf_tensor("kt_fm", [128, 1024], BF16))
        sqs = st1.enter_context(nc.sbuf_tensor("sqs", [128, 1024], F32))
        o_fm = st1.enter_context(nc.sbuf_tensor("o_fm", [128, 1024], BF16))
        S_f = st1.enter_context(nc.sbuf_tensor("S_f", [128, 512], F32))
        S_b = st1.enter_context(nc.sbuf_tensor("S_b", [128, 512], BF16))
        P_s = st1.enter_context(nc.sbuf_tensor("P_s", [128, 2048], BF16))
        PT_s = st1.enter_context(nc.sbuf_tensor("PT_s", [128, 2048], BF16))
        MT_s = st1.enter_context(nc.sbuf_tensor("MT_s", [128, 1024], BF16))
        tmp_s = st1.enter_context(nc.sbuf_tensor("tmp_s", [128, 1024], BF16))
        sm = st1.enter_context(nc.sbuf_tensor("sm", [128, 32], F32))
        ks_s = st1.enter_context(nc.sbuf_tensor("ks_s", [128, 4 * H], F32))
        sb2 = st1.enter_context(nc.sbuf_tensor("sb2", [128, 3 * H], F32))
        st1.close()

        msum = sm[:, 0:1]; negmean = sm[:, 1:2]; m2 = sm[:, 2:3]
        veps = sm[:, 3:4]; lnv = sm[:, 4:5]; rstd = sm[:, 5:6]
        sqsum = sm[:, 6:7]
        m2sum = sm[:, 8:9]; sq2sum = sm[:, 9:10]; m22 = sm[:, 10:11]
        veps2 = sm[:, 11:12]; lnv2 = sm[:, 12:13]
        ksq = ks_s[:, 0:H]; ksq2 = ks_s[:, H:2 * H]
        rr = ks_s[:, 2 * H:3 * H]
        blin = ks_s[:, 3 * H:4 * H]
        beta_v = sb2[:, 0:H]; s1c = sb2[:, H:2 * H]; s2c = sb2[:, 2 * H:3 * H]

        st2 = ExitStack()
        wuT = st2.enter_context(nc.sbuf_tensor("wuT", [128, 8 * HID], BF16))
        wdT = st2.enter_context(nc.sbuf_tensor("wdT", [128, 24 * 1024], BF16))
        x1_in = [st2.enter_context(nc.sbuf_tensor(f"x1i{i}", [128, 1024], F32)) for i in range(2)]
        x1_r = [st2.enter_context(nc.sbuf_tensor(f"x1r{i}", [128, 1024], F32)) for i in range(2)]
        xh2 = st2.enter_context(nc.sbuf_tensor("xh2", [128, 1024], BF16))
        xh2fm = st2.enter_context(nc.sbuf_tensor("xh2fm", [128, 8 * 512], BF16))
        h_fm = st2.enter_context(nc.sbuf_tensor("h_fm", [128, 24 * 512], BF16))
        g_s = [st2.enter_context(nc.sbuf_tensor(f"g_s{i}", [128, 512], BF16)) for i in range(2)]
        y_sb = [st2.enter_context(nc.sbuf_tensor(f"y_sb{i}", [128, 1024], F32)) for i in range(2)]
        st2.close()

        def fmv(tens, h):
            off = (h % 2) * 64
            p = h // 2
            return tens[off:off + 64, p * 128:(p + 1) * 128]

        def Sbv(h):
            off = (h % 2) * 64
            p = h // 2
            return S_b[off:off + 64, p * 64:(p + 1) * 64]

        with nc.Block():
            P = Prog(nc)

            def body():
                E = P.emit
                t_ident = Tok(); t_masks = Tok(); t_bb = Tok()
                t_wgT = Tok(); t_negm2 = Tok(); t_rstd2 = Tok()
                t_lneps = Tok()
                t_pk0, t_pk1 = Tok(), Tok()
                t_pv0, t_pv1 = Tok(), Tok()
                t_pq0, t_pq1 = Tok(), Tok()
                t_p7 = Tok(); t_pT = Tok()
                t_wgb = Tok(); t_wub = Tok(); t_wdb = Tok()
                t_x1s = [Tok() for _ in range(n_tiles)]
                t_wq, t_wk, t_wv, t_wo, t_wb = Tok(), Tok(), Tok(), Tok(), Tok()
                t_xin = [Tok(), Tok()]
                t_x1sb = [Tok(), Tok()]
                t_xhat = Tok(); t_xhfm2 = [Tok(), Tok()]
                t_qtm = Tok(); t_ktm = Tok(); t_ktil = Tok()
                t_vttm = Tok()
                t_qfm = Tok(); t_kfm = Tok(); t_ktfm = Tok()
                t_sqs = Tok(); t_ofm = Tok()
                t_sm = Tok(); t_sm2 = Tok(); t_ks = Tok(); t_sb2 = Tok()
                t_Sf = [Tok() for _ in range(8)]
                t_Sb = [Tok() for _ in range(8)]
                t_P = [Tok() for _ in range(4)]
                t_PT = [Tok() for _ in range(4)]
                t_MT = [Tok() for _ in range(2)]
                t_tmp = [Tok() for _ in range(4)]

                E("sync", lambda e: e.dma_start(out=ident[:, :], in_=id_d[:, :]),
                  writes=[t_ident])
                E("sync", lambda e: e.dma_start(out=bb_bc[:, :], in_=bb_d[:, :]),
                  writes=[t_bb])
                E("sync", lambda e: e.dma_start(out=masks[:, :], in_=mk_d[:, :]),
                  writes=[t_masks])
                E("dve", lambda e: e.memset(lneps[:, :], 1e-5), writes=[t_lneps])

                def wload(dst, src_d, tok, n):
                    E("pool", lambda e: e.dma_start(
                        out=dst[:, :].rearrange("p (c n) -> p c n", n=n),
                        in_=src_d.rearrange("(c p) n -> p c n", p=128)),
                      writes=[tok])
                wload(wq_s, wq_d, t_wq, 1024)
                wload(wk_s, wk_d, t_wk, 1024)
                wload(wv_s, wv_d, t_wv, 1024)
                wload(wb_s, wb_d, t_wb, H)
                wload(wo_s, wo_d, t_wo, 1024)
                if not getattr(Prog, "NO_FFNPREP", False):
                    E("pool", lambda e: e.dma_start(out=wgb_d[:, :], in_=wg_d[:, :]),
                      writes=[t_wgb])
                    E("pool", lambda e: e.dma_start(out=wub_d[:, :], in_=wu_d[:, :]),
                      writes=[t_wub])
                    E("pool", lambda e: e.dma_start(out=wdb_d[:, :], in_=wd_d[:, :]),
                      writes=[t_wdb])

                E("dve", lambda e: e.memset(S_f[:, :], 0.0), writes=t_Sf)
                E("dve", lambda e: e.memset(S_b[:, :], 0.0), writes=t_Sb)

                E("sync", lambda e: e.dma_start(out=x_in[0][:, :], in_=x_d[0:128, :]),
                  writes=[t_xin[0]])

                wgT_calls = list(range(8))

                GP = {0: (t_pk0, pk, 0), 1: (t_pk1, pk, 512)}
                GV = {0: (t_pv0, pv, 0), 1: (t_pv1, pv, 512)}
                GQ = {0: (t_pq0, pq, 0), 1: (t_pq1, pq, 512)}

                def ghds(g):
                    blk, par = g // 2, g % 2
                    return [8 * blk + par + 2 * i for i in range(4)]

                def g_phase(g):
                    gp = g % 2
                    heads = ghds(g)
                    tG, psG, oG = GP[gp]
                    tGT, psGT, oGT = GV[gp]
                    tMT, psMT, oMT = GQ[gp]
                    P.emit_group("pe", [
                        (lambda i, h: lambda e: e.matmul(
                            psG[:, oG + i * 128:oG + (i + 1) * 128],
                            fmv(kt_fm, h), fmv(k_fm, h), start=True, stop=True))(i, h)
                        for i, h in enumerate(heads)],
                      reads=[t_ktfm, t_kfm], writes=[tG])
                    P.emit_group("pe", [
                        (lambda i, h: lambda e: e.matmul(
                            psGT[:, oGT + i * 128:oGT + (i + 1) * 128],
                            fmv(k_fm, h), fmv(kt_fm, h), start=True, stop=True))(i, h)
                        for i, h in enumerate(heads)],
                      reads=[t_ktfm, t_kfm], writes=[tGT])
                    P.emit_group("pe", [
                        (lambda i, h: lambda e: e.matmul(
                            psMT[:, oMT + i * 128:oMT + (i + 1) * 128],
                            fmv(k_fm, h), fmv(q_fm, h), start=True, stop=True))(i, h)
                        for i, h in enumerate(heads)],
                      reads=[t_qfm, t_kfm], writes=[tMT])
                    P.emit_group("pe", [
                        (lambda i, h, gp: lambda e: e.matmul(
                            p7[:, gp * 256 + i * 64:gp * 256 + (i + 1) * 64],
                            fmv(kt_fm, h), Sbv(h), start=True, stop=True))(i, h, gp)
                        for i, h in enumerate(heads)],
                      reads=[t_ktfm] + [t_Sb[h // 2] for h in heads], writes=[t_p7])
                    E("dve", lambda e, gp=gp: e.tensor_tensor(
                        P_s[:, gp * 1024:gp * 1024 + 512],
                        psG[:, oG:oG + 512], mlo, ALU.mult),
                      reads=[tG, t_masks], writes=[t_P[gp * 2 + 0]])
                    E("dve", lambda e, gp=gp: e.tensor_tensor(
                        PT_s[:, gp * 1024:gp * 1024 + 512],
                        psGT[:, oGT:oGT + 512], mup, ALU.mult),
                      reads=[tGT, t_masks], writes=[t_PT[gp * 2 + 0]])
                    E("dve", lambda e, gp=gp: e.tensor_tensor(
                        MT_s[:, gp * 512:(gp + 1) * 512],
                        psMT[:, oMT:oMT + 512], mui, ALU.mult),
                      reads=[tMT, t_masks], writes=[t_MT[gp]])
                    vbase = (8 * (g // 2) + (g % 2)) * 64
                    E("dve", lambda e, gp=gp, vbase=vbase: e.tensor_tensor(
                        tmp_s[:, gp * 512:gp * 512 + 256].rearrange("p (h d) -> p h d", d=64),
                        vt_tm[:, vbase:vbase + 448].rearrange("p (h d) -> p h d", d=64)[:, 0::2, :],
                        p7[:, gp * 256:gp * 256 + 256].rearrange("p (h d) -> p h d", d=64),
                        ALU.subtract),
                      reads=[t_vttm, t_p7], writes=[t_tmp[gp * 2 + 0]])

                def solve_level(g, lv):
                    gp = g % 2
                    lp = lv % 2
                    nlp = 1 - lp
                    tG, psG, oG = GP[gp]
                    tGT, psGT, oGT = GV[gp]
                    tAp, psAp, oAp = GQ[gp]
                    pc = (gp * 2 + lp) * 512
                    npc = (gp * 2 + nlp) * 512
                    tc = (gp * 2 + lp) * 256
                    ntc = (gp * 2 + nlp) * 256
                    P.emit_group("pe", [
                        (lambda i, pc, tc: lambda e: e.matmul(
                            psAp[:, oAp + i * 64:oAp + (i + 1) * 64],
                            PT_s[:, pc + i * 128:pc + (i + 1) * 128],
                            tmp_s[:, tc + i * 64:tc + (i + 1) * 64],
                            start=True, stop=True))(i, pc, tc)
                        for i in range(4)],
                      reads=[t_PT[gp * 2 + lp], t_tmp[gp * 2 + lp]], writes=[tAp])
                    E("dve", lambda e, tc=tc, ntc=ntc: e.tensor_tensor(
                        tmp_s[:, ntc:ntc + 256], tmp_s[:, tc:tc + 256],
                        psAp[:, oAp:oAp + 256], ALU.add),
                      reads=[t_tmp[gp * 2 + lp], tAp], writes=[t_tmp[gp * 2 + nlp]])
                    if lv < 6:
                        P.emit_group("pe", [
                            (lambda i, pc: lambda e: e.matmul(
                                psG[:, oG + i * 128:oG + (i + 1) * 128],
                                PT_s[:, pc + i * 128:pc + (i + 1) * 128],
                                P_s[:, pc + i * 128:pc + (i + 1) * 128],
                                start=True, stop=True))(i, pc)
                            for i in range(4)],
                          reads=[t_P[gp * 2 + lp], t_PT[gp * 2 + lp]], writes=[tG])
                        P.emit_group("pe", [
                            (lambda i, pc: lambda e: e.matmul(
                                psGT[:, oGT + i * 128:oGT + (i + 1) * 128],
                                P_s[:, pc + i * 128:pc + (i + 1) * 128],
                                PT_s[:, pc + i * 128:pc + (i + 1) * 128],
                                start=True, stop=True))(i, pc)
                            for i in range(4)],
                          reads=[t_P[gp * 2 + lp], t_PT[gp * 2 + lp]], writes=[tGT])
                        E("act", lambda e, npc=npc: e.activation(
                            P_s[:, npc:npc + 512], psG[:, oG:oG + 512], AF.Copy),
                          reads=[tG], writes=[t_P[gp * 2 + nlp]])
                        E("dve", lambda e, npc=npc: e.tensor_copy(
                            PT_s[:, npc:npc + 512], psGT[:, oGT:oGT + 512]),
                          reads=[tGT], writes=[t_PT[gp * 2 + nlp]])

                def oi_phase(g):
                    gp = g % 2
                    heads = ghds(g)
                    uc = (gp * 2 + 1) * 256
                    tU = t_tmp[gp * 2 + 1]
                    tAp, psAp, oAp = GQ[gp]
                    ofns = []
                    for i, h in enumerate(heads):
                        p = h // 2
                        off = (h % 2) * 64
                        col = (p % 4) * 128
                        ofns.append((lambda i, off, col, gp, uc: lambda e: e.matmul(
                            p7[off:off + 64, col:col + 128],
                            tmp_s[:, uc + i * 64:uc + (i + 1) * 64],
                            MT_s[:, gp * 512 + i * 128:gp * 512 + (i + 1) * 128],
                            start=True, stop=False))(i, off, col, gp, uc))
                        ofns.append((lambda h, off, col: lambda e: e.matmul(
                            p7[off:off + 64, col:col + 128],
                            Sbv(h), fmv(q_fm, h), start=False, stop=True))(h, off, col))
                    P.emit_group("pe", ofns,
                      reads=[tU, t_MT[gp], t_qfm] + [t_Sb[h // 2] for h in heads],
                      writes=[t_p7])
                    P.emit_group("pe", [
                        (lambda h, i, uc: lambda e: e.matmul(
                            psAp[(h % 2) * 64:(h % 2) * 64 + 64,
                                 oAp + 256 + i * 64:oAp + 256 + (i + 1) * 64],
                            k_tm[:, h * 64:(h + 1) * 64],
                            tmp_s[:, uc + i * 64:uc + (i + 1) * 64],
                            start=True, stop=True))(h, i, uc)
                        for i, h in enumerate(heads)],
                      reads=[t_ktm, tU], writes=[tAp])
                    off = (g % 2) * 64
                    pr0 = (g // 2) * 4
                    prs = [pr0 + j for j in range(4)]
                    E("dve", lambda e, off=off, pr0=pr0: e.tensor_tensor(
                        S_f[off:off + 64, pr0 * 64:(pr0 + 4) * 64],
                        S_f[off:off + 64, pr0 * 64:(pr0 + 4) * 64],
                        psAp[off:off + 64, oAp + 256:oAp + 512], ALU.add),
                      reads=[tAp] + [t_Sf[p2] for p2 in prs],
                      writes=[t_Sf[p2] for p2 in prs])
                    E("act", lambda e, off=off, pr0=pr0: e.activation(
                        S_b[off:off + 64, pr0 * 64:(pr0 + 4) * 64],
                        S_f[off:off + 64, pr0 * 64:(pr0 + 4) * 64], AF.Copy),
                      reads=[t_Sf[p2] for p2 in prs],
                      writes=[t_Sb[p2] for p2 in prs])

                def front(t):
                    bi = t % 2
                    E("dve", lambda e, bi=bi: e.tensor_reduce(msum, x_in[bi][:, :], AXX, ALU.add),
                      reads=[t_xin[bi]], writes=[t_sm])
                    E("act", lambda e, bi=bi: e.activation(sqs[:, :], x_in[bi][:, :], AF.Square,
                                                           accum_out=sqsum),
                      reads=[t_xin[bi]], writes=[t_sqs, t_sm])
                    E("dve", lambda e: e.tensor_scalar(negmean, msum, -1.0 / DM, None, ALU.mult),
                      reads=[t_sm], writes=[t_sm])
                    E("dve", lambda e: e.tensor_tensor(m2, negmean, negmean, ALU.mult),
                      reads=[t_sm], writes=[t_sm])
                    E("dve", lambda e: e.scalar_tensor_tensor(veps, sqsum, 1.0 / DM, m2,
                                                              ALU.mult, ALU.subtract),
                      reads=[t_sm], writes=[t_sm])
                    E("act", lambda e: e.activation(lnv, veps, AF.Ln, bias=lneps[:, :]),
                      reads=[t_sm, t_lneps], writes=[t_sm])
                    E("act", lambda e: e.activation(rstd, lnv, AF.Exp, scale=-0.5),
                      reads=[t_sm], writes=[t_sm])
                    E("dve", lambda e, bi=bi: e.tensor_scalar(xhat[:, :], x_in[bi][:, :],
                                                              negmean, rstd, ALU.add, ALU.mult),
                      reads=[t_xin[bi], t_sm], writes=[t_xhat])
                    P.emit_group("pe", [
                        (lambda cc: lambda e: e.transpose(
                            pT[:, cc * 128:(cc + 1) * 128],
                            xhat[:, cc * 128:(cc + 1) * 128], ident[:, :]))(cc)
                        for cc in range(8)],
                      reads=[t_xhat, t_ident], writes=[t_pT])
                    E("act", lambda e: e.activation(xhfm2[t % 2][:, :], pT[:, :], AF.Copy),
                      reads=[t_pT], writes=[t_xhfm2[t % 2]])

                t00 = 0 if not skip_p1 else n_tiles
                if t00 < n_tiles:
                    front(t00)
                for t in range(t00, n_tiles):
                    bi = t % 2
                    r0 = t * 128
                    xhfm = xhfm2[t % 2]
                    t_xhfm = t_xhfm2[t % 2]
                    if t + 1 < n_tiles:
                        nb = (t + 1) % 2
                        nr = (t + 1) * 128
                        E("sync", lambda e, nb=nb, nr=nr: e.dma_start(
                            out=x_in[nb][:, :], in_=x_d[nr:nr + 128, :]),
                          writes=[t_xin[nb]])
                    if t >= getattr(Prog, 'WGT_T0', 2) and wgT_calls and not getattr(Prog, 'NO_FFNPREP', False):
                        ic = wgT_calls.pop(0)
                        E("sync", lambda e, ic=ic: e.dma_start(
                            out=wgT[:, ic * HID:(ic + 1) * HID],
                            in_=wgb_d[:, ic * 128:(ic + 1) * 128], transpose=True),
                          reads=[t_wgb], writes=[t_wgT])

                    def proj(dst_toks, w_s, t_w, psum, width):
                        if width == 1024:
                            for oc in range(2):
                                P.emit_group("pe", [
                                    (lambda oc, ic, w_s, psum: lambda e: e.matmul(
                                        psum[:, oc * 512:(oc + 1) * 512],
                                        xhfm[:, ic * 128:(ic + 1) * 128],
                                        w_s[:, ic * 1024 + oc * 512: ic * 1024 + (oc + 1) * 512],
                                        start=(ic == 0), stop=(ic == 7)))(oc, ic, w_s, psum)
                                    for ic in range(8)],
                                  reads=[t_xhfm, t_w], writes=[dst_toks[oc]])
                        else:
                            P.emit_group("pe", [
                                (lambda ic, w_s, psum: lambda e: e.matmul(
                                    psum[:, 0:H],
                                    xhfm[:, ic * 128:(ic + 1) * 128],
                                    w_s[:, ic * H:(ic + 1) * H],
                                    start=(ic == 0), stop=(ic == 7)))(ic, w_s, psum)
                                for ic in range(8)],
                              reads=[t_xhfm, t_w], writes=[dst_toks[0]])

                    proj([t_pk0, t_pk1], wk_s, t_wk, pk, 1024)
                    proj([t_p7], wb_s, t_wb, p7, H)
                    proj([t_pv0, t_pv1], wv_s, t_wv, pv, 1024)
                    proj([t_pq0, t_pq1], wq_s, t_wq, pq, 1024)

                    E("dve", lambda e: e.tensor_tensor(blin, p7[:, 0:H], bb_bc[:, :], ALU.add),
                      reads=[t_p7, t_bb], writes=[t_ks])
                    E("act", lambda e: e.activation(ksq2, blin, AF.Exp, scale=-1.0),
                      reads=[t_ks], writes=[t_ks])
                    E("dve", lambda e: e.tensor_scalar(rr, ksq2, 1.0, None, ALU.add),
                      reads=[t_ks], writes=[t_ks])
                    E("dve", lambda e: e.reciprocal(beta_v, rr),
                      reads=[t_ks], writes=[t_sb2])

                    E("act", lambda e: e.activation(sqs[:, 0:512], pk[:, 0:512], AF.Square),
                      reads=[t_pk0], writes=[t_sqs])
                    E("act", lambda e: e.activation(sqs[:, 512:1024], pk[:, 512:1024], AF.Square),
                      reads=[t_pk1], writes=[t_sqs])
                    E("dve", lambda e: e.tensor_reduce(
                        ksq[:, :].unsqueeze(2),
                        sqs[:, :].rearrange("p (h d) -> p h d", d=DH), AXX, ALU.add),
                      reads=[t_sqs], writes=[t_ks])
                    E("dve", lambda e: e.tensor_scalar(ksq2, ksq, 1e-24, None, ALU.max),
                      reads=[t_ks], writes=[t_ks])
                    E("act", lambda e: e.activation(blin, ksq2, AF.Ln),
                      reads=[t_ks], writes=[t_ks])
                    E("act", lambda e: e.activation(rr, blin, AF.Exp, scale=-0.5),
                      reads=[t_ks], writes=[t_ks])
                    E("dve", lambda e: e.tensor_tensor(s2c, beta_v, rr, ALU.mult),
                      reads=[t_sb2, t_ks], writes=[t_sb2])
                    E("dve", lambda e: e.tensor_tensor(s1c, s2c, rr, ALU.mult),
                      reads=[t_sb2, t_ks], writes=[t_sb2])

                    E("act", lambda e: e.activation(k_tm[:, :], pk[:, :], AF.Copy),
                      reads=[t_pk0, t_pk1], writes=[t_ktm])
                    E("dve", lambda e: e.tensor_tensor(
                        ktil[:, :].rearrange("p (h d) -> p h d", d=DH),
                        pk[:, :].rearrange("p (h d) -> p h d", d=DH),
                        s1c.unsqueeze(2).broadcast_to([128, H, DH]), ALU.mult),
                      reads=[t_pk0, t_pk1, t_sb2], writes=[t_ktil])
                    E("dve", lambda e: e.tensor_tensor(
                        vt_tm[:, :].rearrange("p (h d) -> p h d", d=DH),
                        pv[:, :].rearrange("p (h d) -> p h d", d=DH),
                        s2c.unsqueeze(2).broadcast_to([128, H, DH]), ALU.mult),
                      reads=[t_pv0, t_pv1, t_sb2], writes=[t_vttm])
                    E("act", lambda e: e.activation(q_tm[:, :], pq[:, :], AF.Copy),
                      reads=[t_pq0, t_pq1], writes=[t_qtm])

                    for (src, tsrc, dst, tdst) in ((k_tm, t_ktm, k_fm, t_kfm),
                                                   (ktil, t_ktil, kt_fm, t_ktfm),
                                                   (q_tm, t_qtm, q_fm, t_qfm)):
                        P.emit_group("pe", [
                            (lambda p, src: lambda e: e.transpose(
                                pT[:, p * 128:(p + 1) * 128],
                                src[:, p * 128:(p + 1) * 128], ident[:, :]))(p, src)
                            for p in range(8)],
                          reads=[tsrc, t_ident], writes=[t_pT])
                        E("act", lambda e, dst=dst: e.activation(dst[:, :], pT[:, :], AF.Copy),
                          reads=[t_pT], writes=[tdst])

                    for gpair in ((0, 1), (2, 3)):
                        for g in gpair:
                            g_phase(g)
                        for lv in range(7):
                            for g in gpair:
                                solve_level(g, lv)
                        if gpair[0] == 0 and t + 1 < n_tiles:
                            front(t + 1)
                        for g in gpair:
                            oi_phase(g)
                        lo = 0 if gpair[0] == 0 else 512
                        E("act", lambda e, lo=lo: e.activation(
                            o_fm[:, lo:lo + 512], p7[:, 0:512], AF.Copy),
                          reads=[t_p7], writes=[t_ofm])

                    for oc in range(2):
                        tdst = [t_pv0, t_pv1][oc]
                        P.emit_group("pe", [
                            (lambda oc, p: lambda e: e.matmul(
                                pv[:, oc * 512:(oc + 1) * 512],
                                o_fm[:, p * 128:(p + 1) * 128],
                                wo_s[:, p * 1024 + oc * 512:p * 1024 + (oc + 1) * 512],
                                start=(p == 0), stop=(p == 7)))(oc, p)
                            for p in range(8)],
                          reads=[t_ofm, t_wo], writes=[tdst])
                    E("dve", lambda e, bi=bi: e.tensor_tensor(
                        x1_s[bi][:, 0:512], x_in[bi][:, 0:512], pv[:, 0:512], ALU.add),
                      reads=[t_xin[bi], t_pv0], writes=[t_x1sb[bi]])
                    E("dve", lambda e, bi=bi: e.tensor_tensor(
                        x1_s[bi][:, 512:1024], x_in[bi][:, 512:1024], pv[:, 512:1024], ALU.add),
                      reads=[t_xin[bi], t_pv1], writes=[t_x1sb[bi]])

                    E("dve", lambda e, bi=bi: e.tensor_reduce(m2sum, x1_s[bi][:, :], AXX, ALU.add),
                      reads=[t_x1sb[bi]], writes=[t_sm2])
                    E("act", lambda e, bi=bi: e.activation(sqs[:, :], x1_s[bi][:, :], AF.Square,
                                                           accum_out=sq2sum),
                      reads=[t_x1sb[bi]], writes=[t_sqs, t_sm2])
                    E("dve", lambda e, t=t: e.tensor_scalar(
                        negm2[:, t:t + 1], m2sum, -1.0 / DM, None, ALU.mult),
                      reads=[t_sm2], writes=[t_negm2])
                    E("dve", lambda e, t=t: e.tensor_tensor(
                        m22, negm2[:, t:t + 1], negm2[:, t:t + 1], ALU.mult),
                      reads=[t_negm2], writes=[t_sm2])
                    E("dve", lambda e: e.scalar_tensor_tensor(veps2, sq2sum, 1.0 / DM, m22,
                                                              ALU.mult, ALU.subtract),
                      reads=[t_sm2], writes=[t_sm2])
                    E("act", lambda e: e.activation(lnv2, veps2, AF.Ln, bias=lneps[:, :]),
                      reads=[t_sm2, t_lneps], writes=[t_sm2])
                    E("act", lambda e, t=t: e.activation(
                        rstd2[:, t:t + 1], lnv2, AF.Exp, scale=-0.5),
                      reads=[t_sm2], writes=[t_rstd2])

                    E("pool", lambda e, bi=bi, r0=r0: e.dma_start(
                        out=x1_d[r0:r0 + 128, :], in_=x1_s[bi][:, :]),
                      reads=[t_x1sb[bi]], writes=[t_x1s[t]])
                    if dbg:
                        E("pool", lambda e, bi=bi, r0=r0: e.dma_start(
                            out=dbg_d["dbg_x1"][r0:r0 + 128, :], in_=x1_s[bi][:, :]),
                          reads=[t_x1sb[bi]])

                    if (t + 1) % tps == 0 and t + 1 < n_tiles:
                        E("dve", lambda e: e.memset(S_f[:, :], 0.0), writes=t_Sf)
                        E("dve", lambda e: e.memset(S_b[:, :], 0.0), writes=t_Sb)

                while wgT_calls and not getattr(Prog, "NO_FFNPREP", False):
                    ic = wgT_calls.pop(0)
                    E("sync", lambda e, ic=ic: e.dma_start(
                        out=wgT[:, ic * HID:(ic + 1) * HID],
                        in_=wgb_d[:, ic * 128:(ic + 1) * 128], transpose=True),
                      reads=[t_wgb], writes=[t_wgT])

                if dbg:
                    E("pool", lambda e: e.dma_start(out=dbg_d["dbg_x1d"][:, :], in_=x1_d[:, :]),
                      reads=list(t_x1s))
                    E("pool", lambda e: e.dma_start(out=dbg_d["dbg_st"][:, 0:n_tiles], in_=negm2[:, :]),
                      reads=[t_negm2])
                    E("pool", lambda e: e.dma_start(out=dbg_d["dbg_st"][:, n_tiles:2 * n_tiles], in_=rstd2[:, :]),
                      reads=[t_rstd2])

                P.barrier()

                if skip_p2:
                    P.finish()
                    return
                t_wuT = Tok(); t_wdT = Tok()
                t_x1i = [Tok(), Tok()]
                t_x1r = [Tok(), Tok()]
                t_xh2 = Tok(); t_xh2fm = Tok()
                t_hfm = Tok()
                t_gs = [Tok(), Tok()]
                t_y = [Tok(), Tok()]

                for ic in range(8):
                    E("sync", lambda e, ic=ic: e.dma_start(
                        out=wuT[:, ic * HID:(ic + 1) * HID],
                        in_=wub_d[:, ic * 128:(ic + 1) * 128], transpose=True),
                      reads=[t_wub], writes=[t_wuT])
                for hc in range(24):
                    E("sync", lambda e, hc=hc: e.dma_start(
                        out=wdT[:, hc * 1024:(hc + 1) * 1024],
                        in_=wdb_d[:, hc * 128:(hc + 1) * 128], transpose=True),
                      reads=[t_wdb], writes=[t_wdT])

                n_grp = n_tiles // 4
                for G in range(n_grp):
                    t0 = G * 4
                    for sub in range(4):
                        tt = t0 + sub
                        bi = sub % 2
                        E("sync", lambda e, tt=tt, bi=bi: e.dma_start(
                            out=x1_in[bi][:, :], in_=x1_d[tt * 128:(tt + 1) * 128, :]),
                          reads=[t_x1s[tt]], writes=[t_x1i[bi]])
                        E("dve", lambda e, tt=tt, bi=bi: e.tensor_scalar(
                            xh2[:, :], x1_in[bi][:, :],
                            negm2[:, tt:tt + 1], rstd2[:, tt:tt + 1], ALU.add, ALU.mult),
                          reads=[t_x1i[bi], t_negm2, t_rstd2], writes=[t_xh2])
                        P.emit_group("pe", [
                            (lambda cc: lambda e: e.transpose(
                                pT[:, cc * 128:(cc + 1) * 128],
                                xh2[:, cc * 128:(cc + 1) * 128], ident[:, :]))(cc)
                            for cc in range(8)],
                          reads=[t_xh2, t_ident], writes=[t_pT])
                        E("act", lambda e, sub=sub: e.activation(
                            xh2fm[:, :].rearrange("p (a b) -> p a b", b=512)[:, :, sub * 128:(sub + 1) * 128],
                            pT[:, :].rearrange("p (a b) -> p a b", b=128), AF.Copy),
                          reads=[t_pT], writes=[t_xh2fm])

                    for hc in range(24):
                        par = hc % 2
                        tg = [t_pk0, t_pk1][par]
                        tu = [t_pv0, t_pv1][par]
                        og = par * 512
                        P.emit_group("pe", [
                            (lambda hc, ic, og: lambda e: e.matmul(
                                pk[:, og:og + 512],
                                wgT[:, ic * HID + hc * 128:ic * HID + (hc + 1) * 128],
                                xh2fm[:, ic * 512:(ic + 1) * 512],
                                start=(ic == 0), stop=(ic == 7)))(hc, ic, og)
                            for ic in range(8)],
                          reads=[t_wgT, t_xh2fm], writes=[tg])
                        P.emit_group("pe", [
                            (lambda hc, ic, og: lambda e: e.matmul(
                                pv[:, og:og + 512],
                                wuT[:, ic * HID + hc * 128:ic * HID + (hc + 1) * 128],
                                xh2fm[:, ic * 512:(ic + 1) * 512],
                                start=(ic == 0), stop=(ic == 7)))(hc, ic, og)
                            for ic in range(8)],
                          reads=[t_wuT, t_xh2fm], writes=[tu])
                        E("act", lambda e, og=og, par=par: e.activation(
                            g_s[par][:, :], pk[:, og:og + 512], AF.Silu, scale=SC),
                          reads=[tg], writes=[t_gs[par]])
                        E("dve", lambda e, og=og, par=par, hc=hc: e.tensor_tensor(
                            h_fm[:, hc * 512:(hc + 1) * 512],
                            g_s[par][:, :], pv[:, og:og + 512], ALU.mult),
                          reads=[t_gs[par], tu], writes=[t_hfm])

                    for sub in range(4):
                        tt = t0 + sub
                        bi = sub % 2
                        E("sync", lambda e, tt=tt, bi=bi: e.dma_start(
                            out=x1_r[bi][:, :], in_=x1_d[tt * 128:(tt + 1) * 128, :]),
                          reads=[t_x1s[tt]], writes=[t_x1r[bi]])
                        for oc in range(2):
                            tdst = [t_pq0, t_pq1][oc]
                            P.emit_group("pe", [
                                (lambda sub, oc, hc: lambda e: e.matmul(
                                    pq[:, oc * 512:(oc + 1) * 512],
                                    h_fm[:, hc * 512 + sub * 128:hc * 512 + (sub + 1) * 128],
                                    wdT[:, hc * 1024 + oc * 512:hc * 1024 + (oc + 1) * 512],
                                    start=(hc == 0), stop=(hc == 23)))(sub, oc, hc)
                                for hc in range(24)],
                              reads=[t_hfm, t_wdT], writes=[tdst])
                        E("dve", lambda e, bi=bi: e.scalar_tensor_tensor(
                            y_sb[bi][:, 0:512], pq[:, 0:512], SC * SC,
                            x1_r[bi][:, 0:512], ALU.mult, ALU.add),
                          reads=[t_pq0, t_x1r[bi]], writes=[t_y[bi]])
                        E("dve", lambda e, bi=bi: e.scalar_tensor_tensor(
                            y_sb[bi][:, 512:1024], pq[:, 512:1024], SC * SC,
                            x1_r[bi][:, 512:1024], ALU.mult, ALU.add),
                          reads=[t_pq1, t_x1r[bi]], writes=[t_y[bi]])
                        E("pool", lambda e, tt=tt, bi=bi: e.dma_start(
                            out=out_d[tt * 128:(tt + 1) * 128, :], in_=y_sb[bi][:, :]),
                          reads=[t_y[bi]], writes=[])

                P.finish()

            P.run(body)
    return nc


def _host_consts():
    ident = np.eye(128, dtype=BFD)
    ml = np.tril(-np.ones((128, 128), dtype=np.float32), -1).astype(BFD)
    mu = np.triu(-np.ones((128, 128), dtype=np.float32), 1).astype(BFD)
    mi = np.triu(np.ones((128, 128), dtype=np.float32), 0).astype(BFD)
    masks = np.concatenate([np.tile(m, (1, 4)) for m in (ml, mu, mi)], axis=1)
    return ident, np.ascontiguousarray(masks)


def kernel(L0_attn_x, L0_attn_Wq, L0_attn_Wk, L0_attn_Wv, L0_attn_Wbeta,
           L0_attn_b_beta, L0_attn_Wout, L0_ffn_w_gate, L0_ffn_w_up,
           L0_ffn_w_down):
    x = np.ascontiguousarray(np.asarray(L0_attn_x, dtype=np.float32))
    B, T, _ = x.shape
    ident, masks = _host_consts()
    bb_bc = np.ascontiguousarray(
        np.broadcast_to(np.asarray(L0_attn_b_beta, np.float32)[None, :], (128, H)))
    common = {
        "wq": np.ascontiguousarray(np.asarray(L0_attn_Wq, np.float32)),
        "wk": np.ascontiguousarray(np.asarray(L0_attn_Wk, np.float32)),
        "wv": np.ascontiguousarray(np.asarray(L0_attn_Wv, np.float32)),
        "wb": np.ascontiguousarray(np.asarray(L0_attn_Wbeta, np.float32)),
        "wo": np.ascontiguousarray(np.asarray(L0_attn_Wout, np.float32)),
        "wg": np.ascontiguousarray(np.asarray(L0_ffn_w_gate, np.int32)),
        "wu": np.ascontiguousarray(np.asarray(L0_ffn_w_up, np.int32)),
        "wd": np.ascontiguousarray(np.asarray(L0_ffn_w_down, np.int32)),
        "bbc": bb_bc, "cident": ident, "cmasks": masks,
    }
    spc = B // NCORES
    in_maps = []
    for c in range(NCORES):
        m = dict(common)
        m["x"] = np.ascontiguousarray(
            x[c * spc:(c + 1) * spc].reshape(spc * T, DM))
        in_maps.append(m)
    nc = build_nc(n_seq=spc, tps=T // CT)
    res = run_bass_kernel_spmd(nc, in_maps, core_ids=list(range(NCORES)))
    out = np.stack([res.results[c]["out"].reshape(spc, T, DM)
                    for c in range(NCORES)], axis=0)
    return np.ascontiguousarray(out.reshape(B, T, DM))
